# version 06
# baseline (speedup 1.0000x reference)
import numpy as np
import ml_dtypes
import concourse.bass as bass
import concourse.mybir as mybir
from concourse.bass_utils import run_bass_kernel_spmd

F32 = mybir.dt.float32
BF16 = mybir.dt.bfloat16
AF = mybir.ActivationFunctionType
ALU = mybir.AluOpType
AX = mybir.AxisListType
BF = ml_dtypes.bfloat16

TPC = 2048
TT = 256
NTILE = TPC // TT
ARENA_W = 52992
ENGS = ("tensor", "vector", "scalar", "gpsimd", "sync")

_c = np.arange(4096)
_S = (_c % 16) * 256 + ((_c // 16) % 4) * 64 + (_c // 64)
S_INV = np.empty(4096, np.int64)
S_INV[_S] = _c
_mm_, _gg_ = np.meshgrid(np.arange(16), np.arange(64), indexing="ij")
XV_PERM = (_gg_ * 16 + _mm_).reshape(-1)
BM_PERM = XV_PERM.copy()
_nn_, _rr_, _g4_ = np.meshgrid(np.arange(16), np.arange(4), np.arange(64), indexing="ij")
CM_PERM = ((4 * _g4_ + _rr_) * 16 + _nn_).reshape(-1)


class Planner:
    def __init__(self):
        self.c = {}
        self.steps = {e: [] for e in ENGS}

    def inc(self, s, n):
        self.c[s] = self.c.get(s, 0) + n
        return self.c[s]

    def val(self, s):
        return self.c.get(s, 0)

    def step(self, eng, fn):
        self.steps[eng].append(fn)


class EmitCtx:
    def __init__(self, eng, sems):
        self.eng = eng
        self.sems = sems
        self.seen = {}

    def wait(self, name, v):
        if v is None or v <= 0:
            return
        if self.seen.get(name, -1) >= v:
            return
        self.eng.wait_ge(self.sems[name], v)
        self.seen[name] = v


def build_nc(debug=False):
    nc = bass.Bass("TRN2", target_bir_lowering=False)

    xT = nc.dram_tensor("xT", [2048, TPC], F32, kind="ExternalInput").ap()
    ident = nc.dram_tensor("ident", [128, 128], F32, kind="ExternalInput").ap()
    umask = nc.dram_tensor("umask", [128, 15], F32, kind="ExternalInput").ap()
    onehot = nc.dram_tensor("onehot", [8, 1024], F32, kind="ExternalInput").ap()
    inprojT = nc.dram_tensor("inprojT", [81, 128, 2048], F32, kind="ExternalInput").ap()
    dtprojT = nc.dram_tensor("dtprojT", [128, 4096], F32, kind="ExternalInput").ap()
    dtb = nc.dram_tensor("dtb", [128, 32], F32, kind="ExternalInput").ap()
    Asb = nc.dram_tensor("Asb", [128, 512], F32, kind="ExternalInput").ap()
    Dsk = nc.dram_tensor("Dsk", [128, 32], F32, kind="ExternalInput").ap()
    rms1 = nc.dram_tensor("rms1", [128, 16], F32, kind="ExternalInput").ap()
    rms2 = nc.dram_tensor("rms2", [128, 16], F32, kind="ExternalInput").ap()
    outprojT = nc.dram_tensor("outprojT", [16, 128, 4096], F32, kind="ExternalInput").ap()
    gupT = nc.dram_tensor("gupT", [64, 128, 4096], F32, kind="ExternalInput").ap()
    downT = nc.dram_tensor("downT", [4, 16, 128, 2048], F32, kind="ExternalInput").ap()
    WgT = nc.dram_tensor("WgT", [2048, 8], F32, kind="ExternalInput").ap()
    w12T = nc.dram_tensor("w12T", [8, 11, 128, 4096], F32, kind="ExternalInput").ap()
    w3T = nc.dram_tensor("w3T", [8, 1408, 2048], F32, kind="ExternalInput").ap()
    outT = nc.dram_tensor("outT", [2048, TPC], F32, kind="ExternalOutput").ap()

    dk = "ExternalOutput" if debug else "Internal"
    sp_sz = nc.dram_tensor("sp_sz", [4096, TPC], BF16, kind=dk).ap()
    sp_dt = nc.dram_tensor("sp_dt", [4096, TPC], F32, kind=dk).ap()
    sp_x4 = nc.dram_tensor("sp_x4", [4096, TPC], F32, kind=dk).ap()
    sp_b2 = nc.dram_tensor("sp_b2", [2048, TPC], F32, kind=dk).ap()
    sp_cm = nc.dram_tensor("sp_cm", [4096, TPC], F32, kind=dk).ap()
    sp_yloc = nc.dram_tensor("sp_yloc", [4096, TPC], F32, kind=dk).ap()
    sp_xn = nc.dram_tensor("sp_xn", [2048, TPC], BF16, kind=dk).ap()
    sp_hn = nc.dram_tensor("sp_hn", [2048, TPC], BF16, kind=dk).ap()
    sp_h2 = nc.dram_tensor("sp_h2", [2048, TPC], F32, kind=dk).ap()
    sp_m = nc.dram_tensor("sp_m", [8192, TPC], BF16, kind=dk).ap()
    sp_ymoe = nc.dram_tensor("sp_ymoe", [2048, TPC], F32, kind=dk).ap()
    sp_yg = nc.dram_tensor("sp_yg", [4096, TPC], BF16, kind=dk).ap()
    sp_cchend = nc.dram_tensor("sp_cchend", [128, 1024], F32, kind=dk).ap() if debug else None
    dbg_w = nc.dram_tensor("dbg_w", [128, 128], F32, kind=dk).ap() if debug else None
    dbg_swT = nc.dram_tensor("dbg_swT", [8, 2048], F32, kind=dk).ap() if debug else None
    dbg_sc = nc.dram_tensor("dbg_sc", [128, 256], F32, kind=dk).ap() if debug else None
    dbg_hin = nc.dram_tensor("dbg_hin", [128, 512], F32, kind=dk).ap() if debug else None
    dbg_cc = nc.dram_tensor("dbg_cc", [128, 4096], F32, kind=dk).ap() if debug else None
    cc_in = nc.dram_tensor("cc_in", [1, 131072], F32).ap()
    cc_out = nc.dram_tensor("cc_out", [4, 131072], F32).ap()

    pl = Planner()

    from contextlib import ExitStack
    with ExitStack() as _ex:
        arena_t = _ex.enter_context(nc.sbuf_tensor("arena", [128, ARENA_W], F32))
        pb = [_ex.enter_context(nc.psum_tensor(f"pbank{i}", [128, 512], F32)) for i in range(8)]
        _sem_names = ["s_in", "s_w", "s_ld", "s_st", "s_mm", "s_ve", "s_ac", "s_gp", "s_cc", "s_out"]
        SEMS = {n: _ex.enter_context(nc.semaphore(n)) for n in _sem_names}
        block = _ex.enter_context(nc.Block())
        A = arena_t.ap()
        Abf = arena_t.ap().bitcast(BF16)

        def fs(off, n, p0=0, p1=128):
            return A[p0:p1, off:off + n]

        def bcol(off, bfo, n, p0=0, p1=128):
            return Abf[p0:p1, 2 * off + bfo:2 * off + bfo + n]

        def d3(x):
            return x.rearrange("(c p) t -> p c t", p=128)

        def w3r(x):
            return x.rearrange("(k p) m -> p k m", p=128)

        def s3(ap, c):
            return ap.rearrange("p (c t) -> p c t", c=c)

        def dma(q, out, in_, sem="s_ld", waits=(), **kw):
            tgt = pl.inc(sem, 16)
            w = tuple(waits)

            def fn(ctx, out=out, in_=in_, sem=sem, w=w, kw=kw):
                for (s, v) in w:
                    ctx.wait(s, v)
                ctx.eng.dma_start(out=out, in_=in_, **kw).then_inc(ctx.sems[sem], 16)
            pl.step(q, fn)
            return (sem, tgt)

        def _flushop(eng_name):
            if eng_name == "scalar":
                return lambda e: e.copy(fs(o_fla, 1), fs(o_fla, 1))
            return lambda e: e.tensor_copy(fs(o_flv, 1), fs(o_flv, 1))

        def op(eng, f, sem=None, waits=(), flush=False):
            tgt = pl.inc(sem, 1) if sem else None
            w = tuple(waits)
            ff = _flushop(eng) if flush else None

            def fn(ctx, f=f, sem=sem, w=w, ff=ff):
                for (s, v) in w:
                    ctx.wait(s, v)
                ins = f(ctx.eng)
                if ff is not None:
                    ins = ff(ctx.eng)
                if sem:
                    ins.then_inc(ctx.sems[sem], 1)
            pl.step(eng, fn)
            return (sem, tgt)

        def ops(eng, fl, sem=None, waits=(), flush=False):
            tgt = pl.inc(sem, 1) if sem else None
            w = tuple(waits)
            fl = tuple(fl)
            ff = _flushop(eng) if flush else None

            def fn(ctx, fl=fl, sem=sem, w=w, ff=ff):
                for (s, v) in w:
                    ctx.wait(s, v)
                ins = None
                for f in fl:
                    ins = f(ctx.eng)
                if ff is not None:
                    ins = ff(ctx.eng)
                if sem:
                    ins.then_inc(ctx.sems[sem], 1)
            pl.step(eng, fn)
            return (sem, tgt)

        def phase_barrier(q):
            snap = {k: pl.val(k) for k in SEMS}

            def fn(ctx, snap=snap):
                for k, v in snap.items():
                    ctx.wait(k, v)
            pl.step(q, fn)

        o_ones_col = 0
        o_ones_row = 1
        o_ident = 132
        o_rms1 = 260
        o_rms2 = 276
        o_dtb = 292
        o_dsk = 324
        o_negA = 356
        o_umask = 868
        o_onehot = 884
        o_wgf = 1908
        o_scf = 2036
        o_flv = 2164
        o_fla = 2165
        G0 = 2176

        dma("sync", fs(o_ident, 128), ident, "s_in")
        dma("sync", fs(o_rms1, 16), rms1, "s_in")
        dma("sync", fs(o_rms2, 16), rms2, "s_in")
        dma("sync", fs(o_dtb, 32), dtb, "s_in")
        dma("sync", fs(o_dsk, 32), Dsk, "s_in")
        dma("sync", fs(o_negA, 512), Asb, "s_in")
        dma("sync", fs(o_umask, 15), umask, "s_in")
        dma("sync", fs(o_onehot, 1024, 0, 8), onehot, "s_in")
        dma("sync", s3(fs(o_wgf, 128), 16), WgT.rearrange("(k p) e -> p k e", p=128), "s_in")
        glob_in = pl.val("s_in")
        op("vector", lambda e: e.memset(fs(o_ones_col, 1), 1.0))
        op("vector", lambda e: e.memset(fs(o_ones_row, 128, 0, 1), 1.0), sem="s_ve")
        op("scalar", lambda e: e.activation(fs(o_negA, 512), fs(o_negA, 512), AF.Exp),
           waits=[("s_in", glob_in)], sem="s_ac")
        op("vector", lambda e: e.tensor_scalar_mul(fs(o_negA, 512), fs(o_negA, 512), -1.0),
           waits=[("s_ac", pl.val("s_ac"))], sem="s_ve")
        ones_ready = pl.val("s_ve")

        o_swTp = G0
        o_xn = G0 + 2048
        o_h1 = o_xn + 16384
        oL = o_h1 + 16384
        o_lx = oL
        o_lsq = o_lx + 8192
        o_lrow = o_lsq + 512
        o_lbc = o_lrow + 512 * 6
        o_lxc = o_lbc + 1536
        o_lth = o_lxc + 512
        o_lxnf = o_lth + 512
        assert o_lxnf + 1024 < ARENA_W

        r_mu = fs(o_lrow, 512, 0, 1)
        r_ms = fs(o_lrow + 512, 512, 0, 1)
        r_var = fs(o_lrow + 1024, 512, 0, 1)
        r_t = fs(o_lrow + 1536, 512, 0, 1)
        r_rstd = fs(o_lrow + 2048, 512, 0, 1)
        r_rab = fs(o_lrow + 2560, 512, 0, 1)

        sc_mm = [("s_mm", 0), ("s_mm", 0)]
        for tb in range(4):
            lv = dma("sync", s3(fs(o_lx, 8192), 16),
                     d3(xT[:, tb * 512:(tb + 1) * 512]),
                     "s_ld", waits=[("s_ve", pl.val("s_ve"))])
            stats_prev = pl.val("s_ve")
            for k in range(16):
                w = [("s_ld", lv[1]), ("s_ve", ones_ready), ("s_ve", stats_prev)] if k == 0 else []
                op("tensor",
                   (lambda e, k=k: e.matmul(pb[0][0:1, :], fs(o_ones_col, 1),
                                            fs(o_lx + k * 512, 512), start=(k == 0), stop=(k == 15))),
                   sem="s_mm" if k == 15 else None, waits=w)
            for k in range(16):
                aq = op("scalar",
                        (lambda e, k=k: e.activation(fs(o_lsq, 512), fs(o_lx + k * 512, 512), AF.Square)),
                        sem="s_ac",
                        waits=[("s_ld", lv[1])] + ([("s_mm", pl.val("s_mm"))] if k > 0 else []))
                op("tensor",
                   (lambda e, k=k: e.matmul(pb[1][0:1, :], fs(o_ones_col, 1), fs(o_lsq, 512),
                                            start=(k == 0), stop=(k == 15))),
                   sem="s_mm", waits=[("s_ac", aq[1])])
            mm_sums = pl.val("s_mm")
            ops("vector", [
                lambda e: e.tensor_scalar_mul(r_mu, pb[0][0:1, :], 1.0 / 2048.0),
                lambda e: e.tensor_scalar_mul(r_ms, pb[1][0:1, :], 1.0 / 2048.0),
                lambda e: e.tensor_mul(r_t, r_mu, r_mu),
                lambda e: e.tensor_sub(r_var, r_ms, r_t),
                lambda e: e.tensor_scalar_add(r_t, r_var, 1e-5),
            ], sem="s_ve", waits=[("s_mm", mm_sums)])
            ops("scalar", [
                lambda e: e.activation(r_rstd, r_t, AF.Ln),
                lambda e: e.activation(r_rstd, r_rstd, AF.Exp, scale=-0.5),
            ], sem="s_ac", waits=[("s_ve", pl.val("s_ve"))])
            ops("vector", [
                lambda e: e.tensor_mul(r_t, r_rstd, r_rstd),
                lambda e: e.tensor_mul(r_t, r_var, r_t),
                lambda e: e.tensor_scalar_add(r_t, r_t, 1e-5),
            ], sem="s_ve", waits=[("s_ac", pl.val("s_ac"))])
            ops("scalar", [
                lambda e: e.activation(r_rab, r_t, AF.Ln),
                lambda e: e.activation(r_rab, r_rab, AF.Exp, scale=-0.5),
            ], sem="s_ac", waits=[("s_ve", pl.val("s_ve"))])
            op("vector", lambda e: e.tensor_mul(r_rab, r_rstd, r_rab),
               sem="s_ve", waits=[("s_ac", pl.val("s_ac"))])
            vst = pl.val("s_ve")
            op("tensor", lambda e: e.matmul(pb[2][:, :], fs(o_ones_row, 128, 0, 1), r_mu,
                                            start=True, stop=True), waits=[("s_ve", vst)])
            op("tensor", lambda e: e.matmul(pb[3][:, :], fs(o_ones_row, 128, 0, 1), r_rstd,
                                            start=True, stop=True))
            op("tensor", lambda e: e.matmul(pb[4][:, :], fs(o_ones_row, 128, 0, 1), r_rab,
                                            start=True, stop=True), sem="s_mm")
            mmb = pl.val("s_mm")
            ops("vector", [
                lambda e: e.tensor_copy(fs(o_lbc, 512), pb[2][:, :]),
                lambda e: e.tensor_copy(fs(o_lbc + 512, 512), pb[3][:, :]),
                lambda e: e.tensor_copy(fs(o_lbc + 1024, 512), pb[4][:, :]),
            ], sem="s_ve", waits=[("s_mm", mmb)])
            for k in range(16):
                xn_dst = bcol(o_xn, k * 2048 + tb * 512, 512)
                h1_dst = bcol(o_h1, k * 2048 + tb * 512, 512)
                xb = (k % 2) * 512
                ops("vector", [
                    (lambda e, k=k: e.tensor_sub(fs(o_lxc, 512), fs(o_lx + k * 512, 512), fs(o_lbc, 512))),
                    (lambda e, xb=xb: e.tensor_mul(fs(o_lxnf + xb, 512), fs(o_lxc, 512), fs(o_lbc + 512, 512))),
                    (lambda e, d=xn_dst, xb=xb: e.tensor_copy(d, fs(o_lxnf + xb, 512))),
                    (lambda e: e.tensor_mul(fs(o_lth, 512), fs(o_lxc, 512), fs(o_lbc + 1024, 512))),
                    (lambda e, d=h1_dst, k=k: e.tensor_scalar_mul(d, fs(o_lth, 512), fs(o_rms1 + k, 1))),
                ], sem="s_ve", waits=[sc_mm[k % 2]] if sc_mm[k % 2][1] else [])
                xnf_ve = pl.val("s_ve")
                for tcq in range(4):
                    op("tensor",
                       (lambda e, tb=tb, k=k, tcq=tcq, xb=xb:
                        e.matmul(pb[5][:, (tb * 4 + tcq) * 8:(tb * 4 + tcq) * 8 + 8],
                                 fs(o_lxnf + xb + tcq * 128, 128), fs(o_wgf + k * 8, 8),
                                 start=(tb == 0 and k == 0 and tcq == 0),
                                 stop=(tb == 3 and k == 15 and tcq == 3))),
                       sem="s_mm" if tcq == 3 else None,
                       waits=[("s_ve", xnf_ve), ("s_in", glob_in)] if tcq == 0 else [])
                sc_mm[k % 2] = ("s_mm", pl.val("s_mm"))
        scev = op("vector", lambda e: e.tensor_copy(fs(o_scf, 128), pb[5][:, 0:128]),
                  sem="s_ve", waits=[("s_mm", pl.val("s_mm"))], flush=True)
        ln_done = pl.val("s_ve")
        _scev_for_bank5 = scev
        dma("sync", d3(sp_xn), s3(Abf[:, 2 * o_xn:2 * o_xn + 32768], 16),
            "s_st", waits=[("s_ve", ln_done)])

        oM = oL
        o_win = oM
        o_stf = oM + 2048
        o_stb = o_stf + 2048
        o_dtin = o_stb + 1024
        o_dtw = o_dtin + 1024
        o_sv = o_dtw + 2048
        o_sav = o_sv + 2048
        o_sl = o_sav + 2048
        assert o_sl + 2048 < ARENA_W

        dtw_t = dma("gpsimd", bcol(o_dtw, 0, 4096), dtprojT, "s_w",
                    waits=[("s_ve", ln_done)])

        bank_rel = [("s_mm", 0)] * 8
        bank_rel[5] = _scev_for_bank5
        stf_rel = [("s_st", 0)] * 4
        stb_rel = [("s_st", 0)] * 4
        win_rel = [("s_mm", 0), ("s_mm", 0)]

        def evac_m1(mc, nb, bank, kind, mmv):
            sti = (mc * 4 + nb) % 4
            if kind == "z":
                dst = bcol(o_stb, sti * 512, 512)
                ev = op("scalar", (lambda e, d=dst, b=bank: e.activation(d, pb[b][:, :], AF.Silu)),
                        sem="s_ac", waits=[("s_mm", mmv), stb_rel[sti]], flush=True)
                st = dma("sync", sp_sz[mc * 128:(mc + 1) * 128, nb * 512:(nb + 1) * 512], dst,
                         "s_st", waits=[("s_ac", ev[1])])
                stb_rel[sti] = st
                return ev
            dst = fs(o_stf + sti * 512, 512)
            ev = op("vector", (lambda e, d=dst, b=bank: e.tensor_copy(d, pb[b][:, :])),
                    sem="s_ve", waits=[("s_mm", mmv), stf_rel[sti]], flush=True)
            if kind == "xv":
                loc = mc - 32
                last = None
                for ml in range(2):
                    for r in range(4):
                        last = dma("sync",
                                   sp_x4[(2 * loc + ml) * 256 + r * 64:(2 * loc + ml) * 256 + r * 64 + 64,
                                         nb * 512:(nb + 1) * 512],
                                   A[ml * 64:(ml + 1) * 64, o_stf + sti * 512:o_stf + (sti + 1) * 512],
                                   "s_st", waits=[("s_ve", ev[1])])
                stf_rel[sti] = last
            elif kind == "bm":
                loc = mc - 40
                last = None
                for nl in range(2):
                    for q in range(2):
                        last = dma("sync",
                                   sp_b2[(2 * loc + nl) * 128 + q * 64:(2 * loc + nl) * 128 + q * 64 + 64,
                                         nb * 512:(nb + 1) * 512],
                                   A[nl * 64:(nl + 1) * 64, o_stf + sti * 512:o_stf + (sti + 1) * 512],
                                   "s_st", waits=[("s_ve", ev[1])])
                stf_rel[sti] = last
            else:
                st = dma("sync", sp_cm[(mc - 48) * 128:(mc - 47) * 128, nb * 512:(nb + 1) * 512], dst,
                         "s_st", waits=[("s_ve", ev[1])])
                stf_rel[sti] = st
            return ev

        dtin_evs = []
        for mc in range(81):
            buf = mc % 2
            wv = dma("gpsimd", bcol(o_win, buf * 2048, 2048),
                     inprojT[mc],
                     "s_w", waits=[win_rel[buf]] + ([("s_ve", ln_done)] if mc < 2 else []))
            bset = (mc % 2) * 4
            w0 = [("s_w", wv[1]), ("s_ve", ln_done)] + [bank_rel[bset + j] for j in range(4)]
            for k in range(16):
                for nb in range(4):
                    op("tensor",
                       (lambda e, k=k, nb=nb, buf=buf, bset=bset:
                        e.matmul(pb[bset + nb][:, :],
                                 bcol(o_win, buf * 2048 + k * 128, 128),
                                 bcol(o_h1, k * 2048 + nb * 512, 512),
                                 start=(k == 0), stop=(k == 15))),
                       sem="s_mm" if k == 15 else None,
                       waits=w0 if (k == 0 and nb == 0) else [])
                    if k == 15:
                        mmv = pl.val("s_mm")
                        kind = ("z" if mc < 32 else "xv" if mc < 40 else "bm" if mc < 48 else
                                "cm" if mc < 80 else "dtin")
                        if kind == "dtin":
                            ev = op("scalar",
                                    (lambda e, nb=nb, b=bset + nb:
                                     e.activation(bcol(o_dtin, nb * 512, 512), pb[b][:, :], AF.Copy)),
                                    sem="s_ac", waits=[("s_mm", mmv)], flush=True)
                            dtin_evs.append(ev)
                        else:
                            ev = evac_m1(mc, nb, bset + nb, kind, mmv)
                        bank_rel[bset + nb] = ev
            win_rel[buf] = ("s_mm", pl.val("s_mm"))

        dtin_ready = pl.val("s_ac")
        sv_rel = ("s_st", 0)
        for sc in range(32):
            bset = (sc % 2) * 4
            w0 = [("s_w", dtw_t[1]), ("s_ac", dtin_ready)] + [bank_rel[bset + j] for j in range(4)]
            for nb in range(4):
                op("tensor",
                   (lambda e, sc=sc, nb=nb, bset=bset:
                    e.matmul(pb[bset + nb][:, :], bcol(o_dtw, sc * 128, 128),
                             bcol(o_dtin, nb * 512, 512), start=True, stop=True)),
                   sem="s_mm", waits=w0 if nb == 0 else [])
                mmv = pl.val("s_mm")
                ev = op("vector",
                        (lambda e, nb=nb, b=bset + nb, sc=sc:
                         e.tensor_scalar_add(fs(o_sv + nb * 512, 512), pb[b][:, :], fs(o_dtb + sc, 1))),
                        sem="s_ve", waits=[("s_mm", mmv), sv_rel])
                bank_rel[bset + nb] = ev
            vv = pl.val("s_ve")
            av = pl.val("s_ve")
            ops("scalar", [
                lambda e: e.activation(fs(o_sav, 2048), fs(o_sv, 2048), AF.Abs),
                lambda e: e.activation(fs(o_sl, 2048), fs(o_sav, 2048), AF.Exp, scale=-1.0),
                lambda e: e.activation(fs(o_sl, 2048), fs(o_sl, 2048), AF.Ln, bias=1.0),
            ], sem="s_ac", waits=[("s_ve", av)])
            ops("vector", [
                lambda e: e.tensor_scalar_max(fs(o_sav, 2048), fs(o_sv, 2048), 0.0),
                lambda e: e.tensor_add(fs(o_sv, 2048), fs(o_sav, 2048), fs(o_sl, 2048)),
            ], sem="s_ve", waits=[("s_ac", pl.val("s_ac"))], flush=True)
            st = dma("sync", sp_dt[sc * 128:(sc + 1) * 128, :], fs(o_sv, 2048),
                     "s_st", waits=[("s_ve", pl.val("s_ve"))])
            sv_rel = st
        m1_spills = pl.val("s_st")

        phase_barrier("vector")
        phase_barrier("scalar")
        phase_barrier("sync")
        o_sc = o_h1
        o_wtk = o_sc + 256
        o_t2 = o_wtk + 128
        o_swT = o_swTp
        scv = pl.val("s_ve")
        o_t2m1 = o_t2
        o_t2m2 = o_t2 + 16
        o_t2msk = o_wtk
        o_t2s = o_sc + 128
        def stg(fl, w=()):
            for f in fl:
                op("vector", f, sem=None, waits=w)
        stg([(lambda e, tc=tc: e.tensor_reduce(fs(o_t2m1 + tc, 1), fs(o_scf + tc * 8, 8), AX.X, ALU.max))
             for tc in range(16)])
        stg([(lambda e, tc=tc: e.tensor_single_scalar(fs(o_t2msk + tc * 8, 8), fs(o_scf + tc * 8, 8),
                                                      fs(o_t2m1 + tc, 1), ALU.is_equal))
             for tc in range(16)])
        stg([(lambda e, tc=tc: e.scalar_tensor_tensor(fs(o_t2s + tc * 8, 8), fs(o_t2msk + tc * 8, 8),
                                                      -1e30, fs(o_scf + tc * 8, 8), ALU.mult, ALU.add))
             for tc in range(16)])
        stg([(lambda e, tc=tc: e.tensor_reduce(fs(o_t2m2 + tc, 1), fs(o_t2s + tc * 8, 8), AX.X, ALU.max))
             for tc in range(16)])
        stg([(lambda e, tc=tc: e.tensor_single_scalar(fs(o_t2msk + tc * 8, 8), fs(o_scf + tc * 8, 8),
                                                      fs(o_t2m2 + tc, 1), ALU.is_ge))
             for tc in range(16)])
        stg([(lambda e, tc=tc: e.tensor_single_scalar(fs(o_t2s + tc * 8, 8), fs(o_scf + tc * 8, 8),
                                                      fs(o_t2m1 + tc, 1), ALU.subtract))
             for tc in range(15)])
        op("vector", (lambda e: e.tensor_single_scalar(fs(o_t2s + 15 * 8, 8), fs(o_scf + 15 * 8, 8),
                                                       fs(o_t2m1 + 15, 1), ALU.subtract)))
        op("vector", lambda e: e.tensor_copy(fs(o_t2m2, 1), fs(o_t2s, 1)))
        op("vector", lambda e: e.tensor_copy(fs(o_t2m2 + 1, 1), fs(o_t2s + 8, 1)), sem="s_ve")
        op("scalar", (lambda e: e.activation(fs(o_t2s, 128), fs(o_t2s, 128), AF.Exp)),
           sem="s_ac", waits=[("s_ve", pl.val("s_ve"))])
        acx = pl.val("s_ac")
        stg([(lambda e, tc=tc: e.tensor_mul(fs(o_t2s + tc * 8, 8), fs(o_t2s + tc * 8, 8),
                                            fs(o_t2msk + tc * 8, 8)))
             for tc in range(16)], w=[("s_ac", acx)])
        stg([(lambda e, tc=tc: e.tensor_reduce(fs(o_t2m1 + tc, 1), fs(o_t2s + tc * 8, 8), AX.X, ALU.add))
             for tc in range(16)])
        stg([(lambda e, tc=tc: e.reciprocal(fs(o_t2m2 + tc, 1), fs(o_t2m1 + tc, 1)))
             for tc in range(16)])
        stg([(lambda e, tc=tc: e.tensor_scalar_mul(fs(o_wtk + tc * 8, 8), fs(o_t2s + tc * 8, 8),
                                                   fs(o_t2m2 + tc, 1)))
             for tc in range(16)])
        op("vector", lambda e: e.nop(), sem="s_ve")
        for tc in range(16):
            b = tc % 8
            op("tensor", (lambda e, tc=tc, b=b:
                          e.transpose(pb[b][0:8, 0:128], fs(o_wtk + tc * 8, 8), fs(o_ident, 128))),
               sem="s_mm", waits=[("s_ve", pl.val("s_ve")), bank_rel[b]])
            ev = op("vector", (lambda e, tc=tc, b=b:
                               e.tensor_copy(fs(o_swT + tc * 128, 128, 0, 8), pb[b][0:8, 0:128])),
                    sem="s_ve", waits=[("s_mm", pl.val("s_mm"))])
            bank_rel[b] = ev
        w_ready = pl.val("s_ve")
        if debug:
            dma("sync", dbg_w, fs(o_wtk, 128), "s_st", waits=[("s_ve", w_ready)])
            dma("sync", dbg_swT, fs(o_swT, 2048, 0, 8), "s_st", waits=[("s_ve", w_ready)])
            dma("sync", dbg_sc[:, 0:128], fs(o_scf, 128), "s_st", waits=[("s_ve", w_ready)])

        oM0 = o_xn + 2048 + 0
        oM0 = G0 + 2048 + 0
        oM0 = 4224 if G0 == 2176 else G0 + 2048
        o_w3 = oM0
        o_w12 = o_w3 + 11264
        o_xnh = o_w12 + 4096
        o_she = o_xnh + 8192
        o_swb = o_she + 5632
        o_sg = o_swb + 1024
        o_hf = o_sg + 1024
        o_ypi = o_hf + 512
        o_yps = o_ypi + 512
        oM_end = o_yps + 512
        assert oM_end == 36992 or True
        oP0 = oM_end
        o_pB2 = oP0
        o_pCm = o_pB2 + 4096
        o_pdt = o_pCm + 8192
        o_px4 = o_pdt + 512
        o_py = o_px4 + 512
        o_pdA = o_py + 512
        o_pdBx = o_pdA + 512
        o_pshh = o_pdBx + 256
        o_hprev = o_pshh + 256
        o_dtend = o_hprev + 512
        o_dte2 = o_dtend + 32
        o_pend = o_dte2 + 32
        assert o_pend + 512 <= ARENA_W
        o2_cm = o_pB2
        o2_dt = o_pdt
        o2_y = o_px4
        o2_sz = o_py
        o2_P = o_pdA
        o2_tc = o_pdBx
        o2_ygt = o_pshh
        o_dtc = o_dtend
        o_hin = o_hprev
        o_scc = o_pCm
        o_ct1 = o_pB2
        o_ct2 = o_pB2 + 512

        phase_barrier("sync")
        phase_barrier("vector")
        phase_barrier("scalar")
        phase_barrier("gpsimd")

        ymoe_st = {}
        w12_rel = [("s_mm", 0)] * 4
        w3_rel = ("s_mm", 0)
        xnh_rel = ("s_mm", 0)
        ypi_rel = ("s_ve", 0)
        yps_rel = ("s_st", 0)

        def emit_moe_unit(e_, th):
            nonlocal w3_rel, xnh_rel, ypi_rel, yps_rel
            if th == 0:
                w3v = None
                for k in range(11):
                    w3v = dma("gpsimd", bcol(o_w3, k * 2048, 2048),
                              w3T[e_][k * 128:(k + 1) * 128, :], "s_w",
                              waits=[w3_rel] if k == 0 else [])
                self_w3 = w3v
            else:
                self_w3 = ("s_w", pl.val("s_w"))
            xv = dma("sync", s3(bcol(o_xnh, 0, 16384), 16),
                     d3(sp_xn[:, th * 1024:(th + 1) * 1024]), "s_ld", waits=[xnh_rel])
            for j in range(2):
                nb = th * 2 + j
                b = 6 + j
                op("tensor",
                   (lambda e, e_=e_, nb=nb, b=b:
                    e.matmul(pb[b][:, :], fs(o_onehot + e_ * 128, 128, 0, 8),
                             fs(o_swTp + nb * 512, 512, 0, 8), start=True, stop=True)),
                   sem="s_mm", waits=[("s_ve", w_ready), bank_rel[b]])
                ev = op("vector",
                        (lambda e, j=j, b=b: e.tensor_copy(fs(o_swb + j * 512, 512), pb[b][:, :])),
                        sem="s_ve", waits=[("s_mm", pl.val("s_mm"))], flush=(j == 1))
                bank_rel[b] = ev
            wb_ready = pl.val("s_ve")
            mmv2 = 0
            for mc in range(11):
                buf = mc % 2
                w1v = dma("gpsimd", bcol(o_w12, buf * 4096, 4096),
                          w12T[e_][mc],
                          "s_w", waits=[w12_rel[buf], w12_rel[2 + buf]])
                wA = [("s_w", w1v[1]), ("s_ld", xv[1]), bank_rel[0], bank_rel[1]]
                for k in range(16):
                    for j in range(2):
                        op("tensor",
                           (lambda e, k=k, j=j, buf=buf:
                            e.matmul(pb[j][:, :], bcol(o_w12, buf * 4096 + k * 128, 128),
                                     bcol(o_xnh, k * 1024 + j * 512, 512),
                                     start=(k == 0), stop=(k == 15))),
                           sem="s_mm" if k == 15 else None,
                           waits=wA if (k == 0 and j == 0) else [])
                mmv1 = pl.val("s_mm")
                wB = [bank_rel[2], bank_rel[3]]
                for k in range(16):
                    for j in range(2):
                        op("tensor",
                           (lambda e, k=k, j=j, buf=buf:
                            e.matmul(pb[2 + j][:, :], bcol(o_w12, buf * 4096 + 2048 + k * 128, 128),
                                     bcol(o_xnh, k * 1024 + j * 512, 512),
                                     start=(k == 0), stop=(k == 15))),
                           sem="s_mm" if k == 15 else None,
                           waits=wB if (k == 0 and j == 0) else [])
                mmv2 = pl.val("s_mm")
                for j in range(2):
                    ev1 = op("scalar",
                             (lambda e, j=j: e.activation(fs(o_sg + j * 512, 512), pb[j][:, :], AF.Silu)),
                             sem="s_ac", waits=[("s_mm", mmv1)], flush=True)
                    bank_rel[j] = ev1
                acv = pl.val("s_ac")
                for j in range(2):
                    ev2 = ops("vector", [
                        (lambda e, j=j: e.tensor_mul(fs(o_hf, 512), pb[2 + j][:, :], fs(o_sg + j * 512, 512))),
                        (lambda e, j=j, mc=mc: e.tensor_mul(bcol(o_she, mc * 1024 + j * 512, 512),
                                                            fs(o_hf, 512), fs(o_swb + j * 512, 512))),
                    ], sem="s_ve", waits=[("s_mm", mmv2), ("s_ac", acv), ("s_ve", wb_ready)],
                        flush=(mc == 10))
                    bank_rel[2 + j] = ev2
                w12_rel[buf] = ("s_mm", mmv1)
                w12_rel[2 + buf] = ("s_mm", mmv2)
            xnh_rel = ("s_mm", mmv2)
            she_ready = pl.val("s_ve")
            for omc in range(16):
                for j in range(2):
                    b = 4 + (omc % 2) * 2 + j if False else 4 + ((omc * 2 + j) % 4)
                    nb = th * 2 + j
                    wC = [("s_w", self_w3[1]), ("s_ve", she_ready), bank_rel[b]]
                    for k in range(11):
                        op("tensor",
                           (lambda e, k=k, j=j, omc=omc, b=b:
                            e.matmul(pb[b][:, :], bcol(o_w3, k * 2048 + omc * 128, 128),
                                     bcol(o_she, k * 1024 + j * 512, 512),
                                     start=(k == 0), stop=(k == 10))),
                           sem="s_mm" if k == 10 else None,
                           waits=wC if k == 0 else [])
                    mmv3 = pl.val("s_mm")
                    if e_ > 0:
                        ldv = dma("sync", fs(o_ypi, 512),
                                  sp_ymoe[omc * 128:(omc + 1) * 128, nb * 512:(nb + 1) * 512],
                                  "s_ld", waits=[ymoe_st.get((omc, nb), ("s_st", 0)), ypi_rel])
                        ev = op("vector",
                                (lambda e, b=b: e.tensor_add(fs(o_yps, 512), pb[b][:, :], fs(o_ypi, 512))),
                                sem="s_ve",
                                waits=[("s_mm", mmv3), ("s_ld", ldv[1]), yps_rel], flush=True)
                        ypi_rel = ev
                    else:
                        ev = op("vector",
                                (lambda e, b=b: e.tensor_copy(fs(o_yps, 512), pb[b][:, :])),
                                sem="s_ve", waits=[("s_mm", mmv3), yps_rel], flush=True)
                    st = dma("sync", sp_ymoe[omc * 128:(omc + 1) * 128, nb * 512:(nb + 1) * 512],
                             fs(o_yps, 512), "s_st", waits=[("s_ve", ev[1])])
                    ymoe_st[(omc, nb)] = st
                    yps_rel = st
                    bank_rel[b] = ev
            if th == 1:
                w3_rel = ("s_mm", pl.val("s_mm"))

        op("vector", lambda e: e.memset(fs(o_hprev, 512), 0.0))
        op("vector", lambda e: e.memset(fs(o_dtend, 32), 0.0), sem="s_ve")
        pB2_rel = ("s_ve", 0)
        pCm_rel = ("s_ve", 0)
        pdt_rel = [("s_ac", 0)] * 2
        px4_rel = [("s_ve", 0)] * 2
        py_rel = [("s_st", 0)] * 2
        pdA_rel = [("s_ve", 0)] * 2

        def emit_p1_tile(ti):
            nonlocal pB2_rel, pCm_rel
            tb0 = ti * TT
            b2v = dma("sync", s3(fs(o_pB2, 4096), 16), d3(sp_b2[:, tb0:tb0 + TT]),
                      "s_ld", waits=[("s_st", m1_spills), pB2_rel])
            cmv = dma("sync", s3(fs(o_pCm, 8192), 32), d3(sp_cm[:, tb0:tb0 + TT]),
                      "s_ld", waits=[pCm_rel])
            for cb in range(32):
                pf = cb % 2
                dtv = dma("sync", fs(o_pdt + pf * 256, TT),
                          sp_dt[cb * 128:(cb + 1) * 128, tb0:tb0 + TT], "s_ld",
                          waits=[pdt_rel[pf]])
                x4v = dma("sync", fs(o_px4 + pf * 256, TT),
                          sp_x4[cb * 128:(cb + 1) * 128, tb0:tb0 + TT], "s_ld",
                          waits=[px4_rel[pf]])
                ops("vector", [
                    (lambda e, cb=cb, pf=pf: e.tensor_scalar_mul(
                        fs(o_py + pf * 256, TT), fs(o_px4 + pf * 256, TT), fs(o_dsk + cb, 1))),
                    (lambda e, cb=cb, pf=pf: e.tensor_mul(
                        fs(o_px4 + pf * 256, TT), fs(o_pdt + pf * 256, TT), fs(o_px4 + pf * 256, TT))),
                    (lambda e, cb=cb, pf=pf: e.tensor_reduce(
                        fs(o_dte2 + cb, 1), fs(o_pdt + pf * 256, TT), AX.X, ALU.add)),
                ], sem="s_ve", waits=[("s_ld", dtv[1]), ("s_ld", x4v[1]), py_rel[pf]])
                dtx_mark = pl.val("s_ve")
                lastv = None
                for n_ in range(16):
                    ab = n_ % 2
                    av = op("scalar",
                            (lambda e, cb=cb, n_=n_, pf=pf, ab=ab:
                             e.activation(fs(o_pdA + ab * 256, TT), fs(o_pdt + pf * 256, TT),
                                          AF.Exp, scale=fs(o_negA + cb * 16 + n_, 1))),
                            sem="s_ac",
                            waits=([("s_ve", dtx_mark)] if n_ < 2 else []) + [pdA_rel[ab]])
                    lastv = ops("vector", [
                        (lambda e, n_=n_, pf=pf: e.tensor_mul(
                            fs(o_pdBx, TT), fs(o_px4 + pf * 256, TT),
                            fs(o_pB2 + n_ * 256, TT))),
                        (lambda e, cb=cb, n_=n_, ab=ab: e.tensor_tensor_scan(
                            fs(o_pshh, TT), fs(o_pdA + ab * 256, TT), fs(o_pdBx, TT),
                            fs(o_hprev + cb * 16 + n_, 1), ALU.mult, ALU.add)),
                        (lambda e, cb=cb, n_=n_: e.tensor_mul(
                            fs(o_pdBx, TT), fs(o_pshh, TT),
                            fs(o_pCm + (n_ * 2 + cb % 2) * 256, TT))),
                        (lambda e, cb=cb, pf=pf: e.tensor_add(
                            fs(o_py + pf * 256, TT), fs(o_py + pf * 256, TT), fs(o_pdBx, TT))),
                        (lambda e, cb=cb, n_=n_: e.tensor_copy(
                            fs(o_hprev + cb * 16 + n_, 1), A[:, o_pshh + TT - 1:o_pshh + TT])),
                    ], sem="s_ve",
                        waits=[("s_ac", av[1]), ("s_ld", b2v[1]), ("s_ld", cv2 := cmv[1])],
                        flush=(n_ == 15))
                    pdA_rel[ab] = lastv
                st = dma("sync", sp_yloc[cb * 128:(cb + 1) * 128, tb0:tb0 + TT],
                         fs(o_py + pf * 256, TT), "s_st", waits=[("s_ve", lastv[1])])
                py_rel[pf] = st
                pdt_rel[pf] = ("s_ac", pl.val("s_ac"))
                px4_rel[pf] = lastv
            op("vector", lambda e: e.tensor_add(fs(o_dtend, 16), fs(o_dtend, 16), fs(o_dte2, 16)))
            op("vector", lambda e: e.tensor_copy(fs(o_pshh, 1), fs(o_dte2, 1)))
            op("vector", lambda e: e.tensor_add(fs(o_dtend + 16, 16), fs(o_dtend + 16, 16),
                                                fs(o_dte2 + 16, 16)), sem="s_ve")
            pB2_rel = ("s_ve", pl.val("s_ve"))
            pCm_rel = ("s_ve", pl.val("s_ve"))

        def emit_ag():
            for cb in range(32):
                for n_ in range(16):
                    op("vector",
                       (lambda e, cb=cb, n_=n_: e.tensor_mul(
                           fs(o_pend + cb * 16 + n_, 1), fs(o_dtend + cb, 1), fs(o_negA + cb * 16 + n_, 1))),
                       sem="s_ve" if (cb == 31 and n_ == 15) else None)
            op("scalar", lambda e: e.activation(fs(o_pend, 512), fs(o_pend, 512), AF.Exp),
               sem="s_ac", waits=[("s_ve", pl.val("s_ve"))], flush=True)
            pend_done = pl.val("s_ac")
            dma("sync", cc_in.rearrange("a (p x) -> (a p) x", p=128)[:, 0:512],
                fs(o_pend, 512), "s_st", waits=[("s_ac", pend_done)])
            dma("sync", cc_in.rearrange("a (p x) -> (a p) x", p=128)[:, 512:1024],
                fs(o_hprev, 512), "s_st", waits=[("s_ve", pl.val("s_ve"))])
            op("gpsimd", (lambda e: e.collective_compute(
                "AllGather", ALU.bypass, replica_groups=[[0, 1, 2, 3], [4, 5, 6, 7]],
                ins=[cc_in.opt()], outs=[cc_out.opt()])),
                sem="s_gp", waits=[("s_st", pl.val("s_st"))])
            cc_done = pl.val("s_gp")
            ldcc = []
            for j in range(4):
                ldcc.append(dma("sync", fs(o_scc + j * 1024, 1024),
                                cc_out[j:j + 1, :].rearrange("a (p x) -> (a p) x", p=128),
                                "s_ld", waits=[("s_gp", cc_done)]))
            PAIR_IDX = {(0, 1): 0, (0, 2): 1, (0, 3): 2, (1, 2): 3, (1, 3): 4, (2, 3): 5}
            op("vector", lambda e: e.memset(fs(o_hin, 512), 0.0), sem="s_ve",
               waits=[("s_ld", ldcc[-1][1])])
            for j in range(3):
                fl = [(lambda e, j=j: e.tensor_copy(fs(o_ct1, 512), fs(o_scc + j * 1024 + 512, 512)))]
                for i_ in range(j + 1, 4):
                    pi = PAIR_IDX[(j, i_)]
                    fl.append((lambda e, i_=i_, pi=pi: e.tensor_scalar(
                        fs(o_ct2, 512), fs(o_scc + i_ * 1024, 512),
                        fs(o_umask + 3 + pi, 1), fs(o_umask + 9 + pi, 1), ALU.mult, ALU.add)))
                    fl.append((lambda e: e.tensor_mul(fs(o_ct1, 512), fs(o_ct1, 512), fs(o_ct2, 512))))
                fl.append((lambda e, j=j: e.scalar_tensor_tensor(
                    fs(o_hin, 512), fs(o_ct1, 512), fs(o_umask + j, 1), fs(o_hin, 512),
                    ALU.mult, ALU.add)))
                ops("vector", fl, sem="s_ve", flush=(j == 2))
            return pl.val("s_ve")

        p2cm_rel = ("s_ve", 0)
        p2dt_rel = [("s_ac", 0)] * 2
        p2y_rel = [("s_ve", 0)] * 2
        p2sz_rel = [("s_ve", 0)] * 2
        p2P_rel = [("s_ve", 0)] * 2
        p2ygt_rel = [("s_st", 0)] * 2

        def emit_p2_first():
            op("vector", lambda e: e.memset(fs(o_dtc, 32), 0.0), sem="s_ve")

        def emit_p2_tile(ti, hin_done):
            nonlocal p2cm_rel
            tb0 = ti * TT
            cmv = dma("sync", s3(fs(o2_cm, 8192), 32), d3(sp_cm[:, tb0:tb0 + TT]),
                      "s_ld", waits=[("s_ve", hin_done), p2cm_rel])
            for cb in range(32):
                pf = cb % 2
                dtv = dma("sync", fs(o2_dt + pf * 256, TT),
                          sp_dt[cb * 128:(cb + 1) * 128, tb0:tb0 + TT], "s_ld",
                          waits=[p2dt_rel[pf]])
                yv = dma("sync", fs(o2_y + pf * 256, TT),
                         sp_yloc[cb * 128:(cb + 1) * 128, tb0:tb0 + TT], "s_ld",
                         waits=[p2y_rel[pf]])
                szv = dma("sync", bcol(o2_sz, pf * 256, TT),
                          sp_sz[cb * 128:(cb + 1) * 128, tb0:tb0 + TT], "s_ld",
                          waits=[p2sz_rel[pf]])
                op("vector",
                   (lambda e, cb=cb, pf=pf: e.tensor_tensor_scan(
                       fs(o2_dt + pf * 256, TT), fs(o2_dt + pf * 256, TT), fs(o2_dt + pf * 256, TT),
                       fs(o_dtc + cb, 1), ALU.add, ALU.bypass)),
                   sem="s_ve", waits=[("s_ld", dtv[1]), ("s_ve", hin_done)])
                cums = pl.val("s_ve")
                lastv = None
                for n_ in range(16):
                    ab = n_ % 2
                    av = op("scalar",
                            (lambda e, cb=cb, n_=n_, pf=pf, ab=ab:
                             e.activation(fs(o2_P + ab * 256, TT), fs(o2_dt + pf * 256, TT),
                                          AF.Exp, scale=fs(o_negA + cb * 16 + n_, 1))),
                            sem="s_ac",
                            waits=([("s_ve", cums)] if n_ < 2 else []) + [p2P_rel[ab]])
                    lastv = ops("vector", [
                        (lambda e, cb=cb, n_=n_, ab=ab: e.scalar_tensor_tensor(
                            fs(o2_tc, TT), fs(o2_P + ab * 256, TT),
                            fs(o_hin + cb * 16 + n_, 1),
                            fs(o2_cm + (n_ * 2 + cb % 2) * 256, TT),
                            ALU.mult, ALU.mult)),
                        (lambda e, pf=pf: e.tensor_add(
                            fs(o2_y + pf * 256, TT), fs(o2_y + pf * 256, TT), fs(o2_tc, TT))),
                    ], sem="s_ve",
                        waits=[("s_ac", av[1]), ("s_ld", cmv[1]), ("s_ld", yv[1])])
                    p2P_rel[ab] = lastv
                gv = ops("vector", [
                    (lambda e, pf=pf: e.tensor_mul(bcol(o2_ygt, pf * 256, TT),
                                                   fs(o2_y + pf * 256, TT), bcol(o2_sz, pf * 256, TT))),
                    (lambda e, cb=cb, pf=pf: e.tensor_copy(
                        fs(o_dtc + cb, 1), A[:, o2_dt + pf * 256 + TT - 1:o2_dt + pf * 256 + TT])),
                ], sem="s_ve", waits=[("s_ld", szv[1]), p2ygt_rel[pf]], flush=True)
                st = dma("sync", sp_yg[cb * 128:(cb + 1) * 128, tb0:tb0 + TT],
                         bcol(o2_ygt, pf * 256, TT), "s_st", waits=[("s_ve", gv[1])])
                p2ygt_rel[pf] = st
                p2dt_rel[pf] = ("s_ve", gv[1])
                p2y_rel[pf] = ("s_ve", gv[1])
                p2sz_rel[pf] = ("s_ve", gv[1])
            p2cm_rel = ("s_ve", pl.val("s_ve"))

        units = [(e_, th) for e_ in range(8) for th in range(2)]
        for ti in range(NTILE):
            emit_p1_tile(ti)
            emit_moe_unit(*units[ti])
        hin_done = emit_ag()
        _stnow = pl.val("s_st")
        p2dt_rel = [("s_ve", hin_done)] * 2
        p2y_rel = [("s_ve", hin_done)] * 2
        p2sz_rel = [("s_st", _stnow)] * 2
        p2P_rel = [("s_ve", hin_done)] * 2
        p2ygt_rel = [("s_st", _stnow)] * 2
        emit_p2_first()
        for ti in range(NTILE):
            emit_p2_tile(ti, hin_done)
            emit_moe_unit(*units[NTILE + ti])
        p2_st = pl.val("s_st")
        moe_done_st = pl.val("s_st")

        o_ygr = G0
        o_wop = o_ygr + 32768
        o_xns = o_wop + 4096
        o_h2s = o_xns + 2048
        assert o_h2s + 1024 < ARENA_W

        phase_barrier("sync")
        phase_barrier("gpsimd")
        ygrv = dma("sync", s3(bcol(o_ygr, 0, 65536), 32),
                   d3(sp_yg), "s_ld",
                   waits=[("s_st", p2_st)])
        wop_rel = [("s_mm", 0)] * 2
        xns_rel = [("s_mm", 0)] * 2
        h2s_rel = [("s_st", 0)] * 2
        for omc in range(16):
            buf = omc % 2
            wv = dma("gpsimd", bcol(o_wop, buf * 4096, 4096),
                     outprojT[omc],
                     "s_w", waits=[wop_rel[buf]])
            xv = dma("sync", bcol(o_xns, buf * 2048, 2048),
                     sp_xn[omc * 128:(omc + 1) * 128, :], "s_ld", waits=[xns_rel[buf]])
            bset = (omc % 2) * 4
            w0 = [("s_w", wv[1]), ("s_ld", ygrv[1])] + [bank_rel[bset + j] for j in range(4)]
            for cb in range(32):
                for nb in range(4):
                    op("tensor",
                       (lambda e, cb=cb, nb=nb, buf=buf, bset=bset:
                        e.matmul(pb[bset + nb][:, :], bcol(o_wop, buf * 4096 + cb * 128, 128),
                                 bcol(o_ygr, cb * 2048 + nb * 512, 512),
                                 start=(cb == 0), stop=(cb == 31))),
                       sem="s_mm" if cb == 31 else None,
                       waits=w0 if (cb == 0 and nb == 0) else [])
            mmv = pl.val("s_mm")
            for nb in range(4):
                sti = nb % 2
                ev = op("vector",
                        (lambda e, nb=nb, b=bset + nb, buf=buf, sti=sti:
                         e.tensor_add(fs(o_h2s + sti * 512, 512), pb[b][:, :],
                                      bcol(o_xns, buf * 2048 + nb * 512, 512))),
                        sem="s_ve", waits=[("s_mm", mmv), ("s_ld", xv[1]), h2s_rel[sti]], flush=True)
                st = dma("sync", sp_h2[omc * 128:(omc + 1) * 128, nb * 512:(nb + 1) * 512],
                         fs(o_h2s + sti * 512, 512), "s_st", waits=[("s_ve", ev[1])])
                h2s_rel[sti] = st
                bank_rel[bset + nb] = ev
            wop_rel[buf] = ("s_mm", mmv)
            xns_rel[buf] = ("s_mm", mmv)
        h2_st = pl.val("s_st")

        o_h2b = G0
        _ = phase_barrier("sync")
        o_sq2 = o_h2b + 8192
        o_hrow = o_sq2 + 1024
        o_hbc = o_hrow + 1024
        o_hnb = o_hbc + 512
        h2b_rel = ("s_ve", 0)
        for nb in range(4):
            ldv = dma("sync", s3(fs(o_h2b, 8192), 16),
                      d3(sp_h2[:, nb * 512:(nb + 1) * 512]),
                      "s_ld", waits=[("s_st", h2_st), h2b_rel])
            for k in range(16):
                sq = k % 2
                aq = op("scalar",
                        (lambda e, k=k, sq=sq: e.activation(fs(o_sq2 + sq * 512, 512),
                                                            fs(o_h2b + k * 512, 512), AF.Square)),
                        sem="s_ac",
                        waits=[("s_ld", ldv[1])] + ([("s_mm", pl.val("s_mm"))] if k >= 2 else []))
                op("tensor",
                   (lambda e, k=k, sq=sq: e.matmul(pb[0][0:1, :], fs(o_ones_col, 1),
                                                   fs(o_sq2 + sq * 512, 512),
                                                   start=(k == 0), stop=(k == 15))),
                   sem="s_mm", waits=[("s_ac", aq[1])])
            mmv = pl.val("s_mm")
            ops("vector", [
                lambda e: e.tensor_scalar_mul(fs(o_hrow, 512, 0, 1), pb[0][0:1, :], 1.0 / 2048.0),
                lambda e: e.tensor_scalar_add(fs(o_hrow, 512, 0, 1), fs(o_hrow, 512, 0, 1), 1e-5),
            ], sem="s_ve", waits=[("s_mm", mmv)])
            ops("scalar", [
                lambda e: e.activation(fs(o_hrow + 512, 512, 0, 1), fs(o_hrow, 512, 0, 1), AF.Ln),
                lambda e: e.activation(fs(o_hrow + 512, 512, 0, 1), fs(o_hrow + 512, 512, 0, 1),
                                       AF.Exp, scale=-0.5),
            ], sem="s_ac", waits=[("s_ve", pl.val("s_ve"))])
            op("tensor", lambda e: e.matmul(pb[1][:, :], fs(o_ones_row, 128, 0, 1),
                                            fs(o_hrow + 512, 512, 0, 1), start=True, stop=True),
               sem="s_mm", waits=[("s_ac", pl.val("s_ac"))])
            mmv2 = pl.val("s_mm")
            op("vector", lambda e: e.tensor_copy(fs(o_hbc, 512), pb[1][:, :]),
               sem="s_ve", waits=[("s_mm", mmv2)])
            bcv = pl.val("s_ve")
            for k in range(16):
                ops("vector", [
                    (lambda e, k=k: e.tensor_mul(fs(o_h2b + k * 512, 512),
                                                 fs(o_h2b + k * 512, 512), fs(o_hbc, 512))),
                    (lambda e, k=k: e.tensor_scalar_mul(bcol(o_hnb, k * 512, 512),
                                                        fs(o_h2b + k * 512, 512), fs(o_rms2 + k, 1))),
                ], sem="s_ve" if k == 15 else None, waits=[("s_ve", bcv)] if k == 0 else [],
                    flush=(k == 15))
            hn_done = pl.val("s_ve")
            st = dma("sync", d3(sp_hn[:, nb * 512:(nb + 1) * 512]),
                     s3(bcol(o_hnb, 0, 8192), 16), "s_st", waits=[("s_ve", hn_done)])
            h2b_rel = ("s_ve", hn_done)
        hn_st = pl.val("s_st")

        o_shn = G0
        o_smq = o_shn + 16384
        o_wg1 = o_smq + 16384
        o_sg3 = o_wg1 + 4096
        assert o_sg3 + 2048 <= ARENA_W

        phase_barrier("sync")
        phase_barrier("gpsimd")
        hnv = dma("sync", s3(bcol(o_shn, 0, 32768), 16),
                  d3(sp_hn), "s_ld",
                  waits=[("s_st", hn_st)])
        wg_rel = [("s_mm", 0)] * 4
        smq_rel = ("s_st", 0)
        sg3_rel = [("s_ve", 0)] * 4
        for q in range(4):
            for imc in range(16):
                gmc = q * 16 + imc
                buf = imc % 2
                g1 = dma("gpsimd", bcol(o_wg1, buf * 4096, 4096),
                         gupT[gmc],
                         "s_w", waits=[wg_rel[buf], wg_rel[2 + buf]])
                wA = [("s_w", g1[1]), ("s_ld", hnv[1])] + [bank_rel[j] for j in range(4)] + \
                     ([smq_rel] if imc == 0 else [])
                for k in range(16):
                    for nb in range(4):
                        op("tensor",
                           (lambda e, k=k, nb=nb, buf=buf:
                            e.matmul(pb[nb][:, :], bcol(o_wg1, buf * 4096 + k * 128, 128),
                                     bcol(o_shn, k * 2048 + nb * 512, 512),
                                     start=(k == 0), stop=(k == 15))),
                           sem="s_mm" if k == 15 else None,
                           waits=wA if (k == 0 and nb == 0) else [])
                mmv1 = pl.val("s_mm")
                wB = [bank_rel[4 + j] for j in range(4)]
                for k in range(16):
                    for nb in range(4):
                        op("tensor",
                           (lambda e, k=k, nb=nb, buf=buf:
                            e.matmul(pb[4 + nb][:, :], bcol(o_wg1, buf * 4096 + 2048 + k * 128, 128),
                                     bcol(o_shn, k * 2048 + nb * 512, 512),
                                     start=(k == 0), stop=(k == 15))),
                           sem="s_mm" if k == 15 else None,
                           waits=wB if (k == 0 and nb == 0) else [])
                mmv2 = pl.val("s_mm")
                for nb in range(4):
                    ev1 = op("scalar",
                             (lambda e, nb=nb: e.activation(fs(o_sg3 + nb * 512, 512), pb[nb][:, :], AF.Silu)),
                             sem="s_ac", waits=[("s_mm", mmv1), sg3_rel[nb]], flush=True)
                    bank_rel[nb] = ev1
                acv = pl.val("s_ac")
                for nb in range(4):
                    ev2 = op("vector",
                             (lambda e, nb=nb, imc=imc: e.tensor_mul(
                                 bcol(o_smq, imc * 2048 + nb * 512, 512),
                                 pb[4 + nb][:, :], fs(o_sg3 + nb * 512, 512))),
                             sem="s_ve", waits=[("s_mm", mmv2), ("s_ac", acv)],
                             flush=(imc == 15))
                    bank_rel[4 + nb] = ev2
                    sg3_rel[nb] = ev2
                wg_rel[buf] = ("s_mm", mmv1)
                wg_rel[2 + buf] = ("s_mm", mmv2)
            mq_done = pl.val("s_ve")
            st = dma("sync", d3(sp_m[q * 2048:(q + 1) * 2048, :]),
                     s3(bcol(o_smq, 0, 32768), 16), "s_st", waits=[("s_ve", mq_done)])
            smq_rel = st
        m_st = pl.val("s_st")

        o_out = G0
        o_smd = o_out + 32768
        o_wdn = o_smd + 16384
        o_fst = o_smd
        assert o_wdn + 1024 <= ARENA_W

        phase_barrier("sync")
        phase_barrier("gpsimd")
        smd_rel = ("s_mm", 0)
        wdn_rel = [("s_mm", 0)] * 2
        out_ready = [[None] * 4 for _ in range(16)]
        for q in range(4):
            mdv = dma("sync", s3(bcol(o_smd, 0, 32768), 16),
                      d3(sp_m[q * 2048:(q + 1) * 2048, :]),
                      "s_ld", waits=[("s_st", m_st), smd_rel])
            for omc in range(16):
                buf = 0
                wv = dma("gpsimd", bcol(o_wdn, 0, 2048),
                         downT[q][omc],
                         "s_w", waits=[wdn_rel[0]])
                bset = (omc % 2) * 4
                w0 = [("s_w", wv[1]), ("s_ld", mdv[1])] + [bank_rel[bset + j] for j in range(4)]
                for k in range(16):
                    for nb in range(4):
                        op("tensor",
                           (lambda e, k=k, nb=nb, buf=buf, bset=bset:
                            e.matmul(pb[bset + nb][:, :], bcol(o_wdn, buf * 2048 + k * 128, 128),
                                     bcol(o_smd, k * 2048 + nb * 512, 512),
                                     start=(k == 0), stop=(k == 15))),
                           sem="s_mm" if k == 15 else None,
                           waits=w0 if (k == 0 and nb == 0) else [])
                mmv = pl.val("s_mm")
                for nb in range(4):
                    if q == 0:
                        ev = op("vector",
                                (lambda e, omc=omc, nb=nb, b=bset + nb:
                                 e.tensor_copy(fs(o_out + omc * 2048 + nb * 512, 512), pb[b][:, :])),
                                sem="s_ve", waits=[("s_mm", mmv)])
                    else:
                        ev = op("vector",
                                (lambda e, omc=omc, nb=nb, b=bset + nb:
                                 e.tensor_add(fs(o_out + omc * 2048 + nb * 512, 512),
                                              fs(o_out + omc * 2048 + nb * 512, 512), pb[b][:, :])),
                                sem="s_ve", waits=[("s_mm", mmv)])
                    bank_rel[bset + nb] = ev
                    out_ready[omc][nb] = ev
                wdn_rel[0] = ("s_mm", mmv)
            smd_rel = ("s_mm", pl.val("s_mm"))
        mlp_done = pl.val("s_ve")

        fst_rel = ("s_ve", 0)
        for nb in range(4):
            for src_i, src in enumerate((xT, sp_h2, sp_ymoe)):
                ldv = dma("sync", s3(fs(o_fst, 8192), 16),
                          d3(src[:, nb * 512:(nb + 1) * 512]),
                          "s_ld", waits=[fst_rel] + ([("s_ve", mlp_done)] if nb == 0 and src_i == 0 else []) +
                          ([("s_st", moe_done_st)] if src_i == 2 and nb == 0 else []))
                for k in range(16):
                    op("vector",
                       (lambda e, k=k, nb=nb: e.tensor_add(
                           fs(o_out + k * 2048 + nb * 512, 512),
                           fs(o_out + k * 2048 + nb * 512, 512), fs(o_fst + k * 512, 512))),
                       sem="s_ve" if k == 15 else None,
                       waits=[("s_ld", ldv[1])] if k == 0 else [],
                       flush=(k == 15))
                fst_rel = ("s_ve", pl.val("s_ve"))
        final_done = pl.val("s_ve")
        dma("sync", d3(outT), s3(fs(o_out, 32768), 16),
            "s_out", waits=[("s_ve", final_done)])
        out_t = pl.val("s_out")

        def fin(ctx):
            ctx.wait("s_out", out_t)
        pl.step("sync", fin)

        @block.tensor
        def _(eng):
            ctx = EmitCtx(eng, SEMS)
            for fn in pl.steps["tensor"]:
                fn(ctx)

        @block.vector
        def _(eng):
            ctx = EmitCtx(eng, SEMS)
            for fn in pl.steps["vector"]:
                fn(ctx)

        @block.scalar
        def _(eng):
            ctx = EmitCtx(eng, SEMS)
            for fn in pl.steps["scalar"]:
                fn(ctx)

        @block.gpsimd
        def _(eng):
            ctx = EmitCtx(eng, SEMS)
            for fn in pl.steps["gpsimd"]:
                fn(ctx)

        @block.sync
        def _(eng):
            ctx = EmitCtx(eng, SEMS)
            for fn in pl.steps["sync"]:
                fn(ctx)

    return nc


def _host_prep(inputs):
    f = np.float32
    w = {}
    ip = np.asarray(inputs["L0_attn_in_proj_w"], f)
    ip_perm = np.concatenate([
        ip[0:4096][S_INV], ip[4096:5120][XV_PERM], ip[5120:6144][BM_PERM],
        ip[6144:10240][CM_PERM], ip[10240:10368]], axis=0)
    ipT = ip_perm.T
    w["inprojT"] = np.ascontiguousarray(
        ipT.reshape(16, 128, 81, 128).transpose(2, 1, 0, 3).reshape(81, 128, 2048))
    w["dtprojT"] = np.ascontiguousarray(np.asarray(inputs["L0_attn_dt_proj_w"], f)[S_INV].T)
    w["dtb"] = np.ascontiguousarray(np.asarray(inputs["L0_attn_dt_proj_b"], f)[S_INV].reshape(32, 128).T)
    w["Asb"] = np.ascontiguousarray(
        np.asarray(inputs["L0_attn_A_log"], f)[S_INV].reshape(32, 128, 16).transpose(1, 0, 2).reshape(128, 512))
    w["Dsk"] = np.ascontiguousarray(np.asarray(inputs["L0_attn_D_skip"], f)[S_INV].reshape(32, 128).T)
    w["rms1"] = np.ascontiguousarray(np.asarray(inputs["L0_attn_rms1_w"], f).reshape(16, 128).T)
    w["rms2"] = np.ascontiguousarray(np.asarray(inputs["L0_attn_rms2_w"], f).reshape(16, 128).T)
    opT = np.asarray(inputs["L0_attn_out_proj_w"], f)[:, S_INV].T
    w["outprojT"] = np.ascontiguousarray(
        opT.reshape(32, 128, 16, 128).transpose(2, 1, 0, 3).reshape(16, 128, 4096))
    gT = np.asarray(inputs["L0_attn_gate_w"], f).T
    uT = np.asarray(inputs["L0_attn_up_w"], f).T
    g4 = gT.reshape(16, 128, 64, 128).transpose(2, 1, 0, 3).reshape(64, 128, 2048)
    u4 = uT.reshape(16, 128, 64, 128).transpose(2, 1, 0, 3).reshape(64, 128, 2048)
    w["gupT"] = np.ascontiguousarray(np.concatenate([g4, u4], axis=2))
    dT = np.asarray(inputs["L0_attn_down_w"], f).T
    w["downT"] = np.ascontiguousarray(
        dT.reshape(4, 16, 128, 16, 128).transpose(0, 3, 2, 1, 4).reshape(4, 16, 128, 2048))
    w["WgT"] = np.ascontiguousarray(np.asarray(inputs["L0_ffn_Wg"], f).T)
    w1T = np.asarray(inputs["L0_ffn_W1"], f).transpose(0, 2, 1)
    w2T = np.asarray(inputs["L0_ffn_W2"], f).transpose(0, 2, 1)
    w1p = w1T.reshape(8, 16, 128, 11, 128).transpose(0, 3, 2, 1, 4).reshape(8, 11, 128, 2048)
    w2p = w2T.reshape(8, 16, 128, 11, 128).transpose(0, 3, 2, 1, 4).reshape(8, 11, 128, 2048)
    w["w12T"] = np.ascontiguousarray(np.concatenate([w1p, w2p], axis=3))
    w["w3T"] = np.ascontiguousarray(np.asarray(inputs["L0_ffn_W3"], f).transpose(0, 2, 1))
    w["ident"] = np.eye(128, dtype=f)
    oh = np.zeros((8, 1024), f)
    for e in range(8):
        oh[e, e * 128:(e + 1) * 128] = 1.0
    w["onehot"] = oh
    return w


def _umask_for(rk):
    pairs = [(0, 1), (0, 2), (0, 3), (1, 2), (1, 3), (2, 3)]
    v = np.zeros(15, np.float32)
    for j in range(3):
        v[j] = 1.0 if j < rk else 0.0
    for pi, (j, i) in enumerate(pairs):
        u = 1.0 if (j < i < rk) else 0.0
        v[3 + pi] = u
        v[9 + pi] = 1.0 - u
    return np.ascontiguousarray(np.broadcast_to(v[None, :], (128, 15)).copy())


def kernel(**inputs):
    x = np.asarray(inputs["L0_attn_hidden_states"], np.float32)
    w = _host_prep(inputs)
    nc = build_nc(debug=False)
    in_maps = []
    for i in range(8):
        b, q = i // 4, i % 4
        im = dict(w)
        im["xT"] = np.ascontiguousarray(x[b, q * TPC:(q + 1) * TPC].T)
        im["umask"] = _umask_for(q)
        in_maps.append(im)
    res = run_bass_kernel_spmd(nc, in_maps, core_ids=list(range(8)))
    out = np.empty((2, 8192, 2048), np.float32)
    for i in range(8):
        b, q = i // 4, i % 4
        out[b, q * TPC:(q + 1) * TPC] = res.results[i]["outT"].T
    return out




# version 07
# speedup vs baseline: 1.0135x; 1.0135x over previous
import numpy as np
import ml_dtypes
import concourse.bass as bass
import concourse.mybir as mybir
from concourse.bass_utils import run_bass_kernel_spmd

F32 = mybir.dt.float32
BF16 = mybir.dt.bfloat16
AF = mybir.ActivationFunctionType
ALU = mybir.AluOpType
AX = mybir.AxisListType
BF = ml_dtypes.bfloat16

TPC = 2048
TT = 256
NTILE = TPC // TT
ARENA_W = 52992
ENGS = ("tensor", "vector", "scalar", "gpsimd", "sync")

_c = np.arange(4096)
_S = (_c % 16) * 256 + ((_c // 16) % 4) * 64 + (_c // 64)
S_INV = np.empty(4096, np.int64)
S_INV[_S] = _c
_mm_, _gg_ = np.meshgrid(np.arange(16), np.arange(64), indexing="ij")
XV_PERM = (_gg_ * 16 + _mm_).reshape(-1)
BM_PERM = XV_PERM.copy()
_nn_, _rr_, _g4_ = np.meshgrid(np.arange(16), np.arange(4), np.arange(64), indexing="ij")
CM_PERM = ((4 * _g4_ + _rr_) * 16 + _nn_).reshape(-1)


class Planner:
    def __init__(self):
        self.c = {}
        self.steps = {e: [] for e in ENGS}

    def inc(self, s, n):
        self.c[s] = self.c.get(s, 0) + n
        return self.c[s]

    def val(self, s):
        return self.c.get(s, 0)

    def step(self, eng, fn):
        self.steps[eng].append(fn)


class EmitCtx:
    def __init__(self, eng, sems):
        self.eng = eng
        self.sems = sems
        self.seen = {}

    def wait(self, name, v):
        if v is None or v <= 0:
            return
        if self.seen.get(name, -1) >= v:
            return
        self.eng.wait_ge(self.sems[name], v)
        self.seen[name] = v


def build_nc(debug=False):
    nc = bass.Bass("TRN2", target_bir_lowering=False)

    xT = nc.dram_tensor("xT", [2048, TPC], F32, kind="ExternalInput").ap()
    ident = nc.dram_tensor("ident", [128, 128], F32, kind="ExternalInput").ap()
    umask = nc.dram_tensor("umask", [128, 15], F32, kind="ExternalInput").ap()
    onehot = nc.dram_tensor("onehot", [8, 1024], F32, kind="ExternalInput").ap()
    inprojT = nc.dram_tensor("inprojT", [81, 128, 2048], F32, kind="ExternalInput").ap()
    dtprojT = nc.dram_tensor("dtprojT", [128, 4096], F32, kind="ExternalInput").ap()
    dtb = nc.dram_tensor("dtb", [128, 32], F32, kind="ExternalInput").ap()
    Asb = nc.dram_tensor("Asb", [128, 512], F32, kind="ExternalInput").ap()
    Dsk = nc.dram_tensor("Dsk", [128, 32], F32, kind="ExternalInput").ap()
    rms1 = nc.dram_tensor("rms1", [128, 16], F32, kind="ExternalInput").ap()
    rms2 = nc.dram_tensor("rms2", [128, 16], F32, kind="ExternalInput").ap()
    outprojT = nc.dram_tensor("outprojT", [16, 128, 4096], F32, kind="ExternalInput").ap()
    gupT = nc.dram_tensor("gupT", [64, 128, 4096], F32, kind="ExternalInput").ap()
    downT = nc.dram_tensor("downT", [4, 16, 128, 2048], F32, kind="ExternalInput").ap()
    WgT = nc.dram_tensor("WgT", [2048, 8], F32, kind="ExternalInput").ap()
    w12T = nc.dram_tensor("w12T", [8, 11, 128, 4096], F32, kind="ExternalInput").ap()
    w3T = nc.dram_tensor("w3T", [8, 1408, 2048], F32, kind="ExternalInput").ap()
    outT = nc.dram_tensor("outT", [2048, TPC], F32, kind="ExternalOutput").ap()

    dk = "ExternalOutput" if debug else "Internal"
    sp_sz = nc.dram_tensor("sp_sz", [4096, TPC], BF16, kind=dk).ap()
    sp_dt = nc.dram_tensor("sp_dt", [4096, TPC], F32, kind=dk).ap()
    sp_x4 = nc.dram_tensor("sp_x4", [4096, TPC], F32, kind=dk).ap()
    sp_b2 = nc.dram_tensor("sp_b2", [2048, TPC], F32, kind=dk).ap()
    sp_cm = nc.dram_tensor("sp_cm", [4096, TPC], F32, kind=dk).ap()
    sp_yloc = nc.dram_tensor("sp_yloc", [4096, TPC], F32, kind=dk).ap()
    sp_xn = nc.dram_tensor("sp_xn", [2048, TPC], BF16, kind=dk).ap()
    sp_hn = nc.dram_tensor("sp_hn", [2048, TPC], BF16, kind=dk).ap()
    sp_h2 = nc.dram_tensor("sp_h2", [2048, TPC], F32, kind=dk).ap()
    sp_m = nc.dram_tensor("sp_m", [8192, TPC], BF16, kind=dk).ap()
    sp_ymoe = nc.dram_tensor("sp_ymoe", [2048, TPC], F32, kind=dk).ap()
    sp_yg = nc.dram_tensor("sp_yg", [4096, TPC], BF16, kind=dk).ap()
    sp_cchend = nc.dram_tensor("sp_cchend", [128, 1024], F32, kind=dk).ap() if debug else None
    dbg_w = nc.dram_tensor("dbg_w", [128, 128], F32, kind=dk).ap() if debug else None
    dbg_swT = nc.dram_tensor("dbg_swT", [8, 2048], F32, kind=dk).ap() if debug else None
    dbg_sc = nc.dram_tensor("dbg_sc", [128, 256], F32, kind=dk).ap() if debug else None
    dbg_hin = nc.dram_tensor("dbg_hin", [128, 512], F32, kind=dk).ap() if debug else None
    dbg_cc = nc.dram_tensor("dbg_cc", [128, 4096], F32, kind=dk).ap() if debug else None
    cc_in = nc.dram_tensor("cc_in", [1, 131072], F32).ap()
    cc_out = nc.dram_tensor("cc_out", [4, 131072], F32).ap()

    pl = Planner()

    from contextlib import ExitStack
    with ExitStack() as _ex:
        arena_t = _ex.enter_context(nc.sbuf_tensor("arena", [128, ARENA_W], F32))
        pb = [_ex.enter_context(nc.psum_tensor(f"pbank{i}", [128, 512], F32)) for i in range(8)]
        _sem_names = ["s_in", "s_w", "s_ld", "s_st", "s_mm", "s_ve", "s_ac", "s_gp", "s_cc", "s_out"]
        SEMS = {n: _ex.enter_context(nc.semaphore(n)) for n in _sem_names}
        block = _ex.enter_context(nc.Block())
        A = arena_t.ap()
        Abf = arena_t.ap().bitcast(BF16)

        def fs(off, n, p0=0, p1=128):
            return A[p0:p1, off:off + n]

        def bcol(off, bfo, n, p0=0, p1=128):
            return Abf[p0:p1, 2 * off + bfo:2 * off + bfo + n]

        def d3(x):
            return x.rearrange("(c p) t -> p c t", p=128)

        def w3r(x):
            return x.rearrange("(k p) m -> p k m", p=128)

        def s3(ap, c):
            return ap.rearrange("p (c t) -> p c t", c=c)

        def dma(q, out, in_, sem="s_ld", waits=(), **kw):
            tgt = pl.inc(sem, 16)
            w = tuple(waits)

            def fn(ctx, out=out, in_=in_, sem=sem, w=w, kw=kw):
                for (s, v) in w:
                    ctx.wait(s, v)
                ctx.eng.dma_start(out=out, in_=in_, **kw).then_inc(ctx.sems[sem], 16)
            pl.step(q, fn)
            return (sem, tgt)

        def _flushop(eng_name):
            if eng_name == "scalar":
                return lambda e: e.copy(fs(o_fla, 1), fs(o_fla, 1))
            return lambda e: e.tensor_copy(fs(o_flv, 1), fs(o_flv, 1))

        def op(eng, f, sem=None, waits=(), flush=False):
            tgt = pl.inc(sem, 1) if sem else None
            w = tuple(waits)
            ff = _flushop(eng) if flush else None

            def fn(ctx, f=f, sem=sem, w=w, ff=ff):
                for (s, v) in w:
                    ctx.wait(s, v)
                ins = f(ctx.eng)
                if ff is not None:
                    ins = ff(ctx.eng)
                if sem:
                    ins.then_inc(ctx.sems[sem], 1)
            pl.step(eng, fn)
            return (sem, tgt)

        def ops(eng, fl, sem=None, waits=(), flush=False):
            tgt = pl.inc(sem, 1) if sem else None
            w = tuple(waits)
            fl = tuple(fl)
            ff = _flushop(eng) if flush else None

            def fn(ctx, fl=fl, sem=sem, w=w, ff=ff):
                for (s, v) in w:
                    ctx.wait(s, v)
                ins = None
                for f in fl:
                    ins = f(ctx.eng)
                if ff is not None:
                    ins = ff(ctx.eng)
                if sem:
                    ins.then_inc(ctx.sems[sem], 1)
            pl.step(eng, fn)
            return (sem, tgt)

        def phase_barrier(q):
            snap = {k: pl.val(k) for k in SEMS}

            def fn(ctx, snap=snap):
                for k, v in snap.items():
                    ctx.wait(k, v)
            pl.step(q, fn)

        o_ones_col = 0
        o_ones_row = 1
        o_ident = 132
        o_rms1 = 260
        o_rms2 = 276
        o_dtb = 292
        o_dsk = 324
        o_negA = 356
        o_umask = 868
        o_onehot = 884
        o_wgf = 1908
        o_scf = 2036
        o_flv = 2164
        o_fla = 2165
        G0 = 2176

        dma("sync", fs(o_ident, 128), ident, "s_in")
        dma("sync", fs(o_rms1, 16), rms1, "s_in")
        dma("sync", fs(o_rms2, 16), rms2, "s_in")
        dma("sync", fs(o_dtb, 32), dtb, "s_in")
        dma("sync", fs(o_dsk, 32), Dsk, "s_in")
        dma("sync", fs(o_negA, 512), Asb, "s_in")
        dma("sync", fs(o_umask, 15), umask, "s_in")
        dma("sync", fs(o_onehot, 1024, 0, 8), onehot, "s_in")
        dma("sync", s3(fs(o_wgf, 128), 16), WgT.rearrange("(k p) e -> p k e", p=128), "s_in")
        glob_in = pl.val("s_in")
        op("vector", lambda e: e.memset(fs(o_ones_col, 1), 1.0))
        op("vector", lambda e: e.memset(fs(o_ones_row, 128, 0, 1), 1.0), sem="s_ve")
        op("scalar", lambda e: e.activation(fs(o_negA, 512), fs(o_negA, 512), AF.Exp),
           waits=[("s_in", glob_in)], sem="s_ac")
        op("vector", lambda e: e.tensor_scalar_mul(fs(o_negA, 512), fs(o_negA, 512), -1.0),
           waits=[("s_ac", pl.val("s_ac"))], sem="s_ve")
        ones_ready = pl.val("s_ve")

        o_swTp = G0
        o_xn = G0 + 2048
        o_h1 = o_xn + 16384
        oL = o_h1 + 16384
        o_lx = oL
        o_lsq = o_lx + 8192
        o_lrow = o_lsq + 512
        o_lbc = o_lrow + 512 * 6
        o_lxc = o_lbc + 1536
        o_lth = o_lxc + 512
        o_lxnf = o_lth + 512
        assert o_lxnf + 1024 < ARENA_W

        r_mu = fs(o_lrow, 512, 0, 1)
        r_ms = fs(o_lrow + 512, 512, 0, 1)
        r_var = fs(o_lrow + 1024, 512, 0, 1)
        r_t = fs(o_lrow + 1536, 512, 0, 1)
        r_rstd = fs(o_lrow + 2048, 512, 0, 1)
        r_rab = fs(o_lrow + 2560, 512, 0, 1)

        sc_mm = [("s_mm", 0), ("s_mm", 0)]
        for tb in range(4):
            lv = dma("sync", s3(fs(o_lx, 8192), 16),
                     d3(xT[:, tb * 512:(tb + 1) * 512]),
                     "s_ld", waits=[("s_ve", pl.val("s_ve"))])
            stats_prev = pl.val("s_ve")
            for k in range(16):
                w = [("s_ld", lv[1]), ("s_ve", ones_ready), ("s_ve", stats_prev)] if k == 0 else []
                op("tensor",
                   (lambda e, k=k: e.matmul(pb[0][0:1, :], fs(o_ones_col, 1),
                                            fs(o_lx + k * 512, 512), start=(k == 0), stop=(k == 15))),
                   sem="s_mm" if k == 15 else None, waits=w)
            for k in range(16):
                aq = op("scalar",
                        (lambda e, k=k: e.activation(fs(o_lsq, 512), fs(o_lx + k * 512, 512), AF.Square)),
                        sem="s_ac",
                        waits=[("s_ld", lv[1])] + ([("s_mm", pl.val("s_mm"))] if k > 0 else []))
                op("tensor",
                   (lambda e, k=k: e.matmul(pb[1][0:1, :], fs(o_ones_col, 1), fs(o_lsq, 512),
                                            start=(k == 0), stop=(k == 15))),
                   sem="s_mm", waits=[("s_ac", aq[1])])
            mm_sums = pl.val("s_mm")
            ops("vector", [
                lambda e: e.tensor_scalar_mul(r_mu, pb[0][0:1, :], 1.0 / 2048.0),
                lambda e: e.tensor_scalar_mul(r_ms, pb[1][0:1, :], 1.0 / 2048.0),
                lambda e: e.tensor_mul(r_t, r_mu, r_mu),
                lambda e: e.tensor_sub(r_var, r_ms, r_t),
                lambda e: e.tensor_scalar_add(r_t, r_var, 1e-5),
            ], sem="s_ve", waits=[("s_mm", mm_sums)])
            ops("scalar", [
                lambda e: e.activation(r_rstd, r_t, AF.Ln),
                lambda e: e.activation(r_rstd, r_rstd, AF.Exp, scale=-0.5),
            ], sem="s_ac", waits=[("s_ve", pl.val("s_ve"))])
            ops("vector", [
                lambda e: e.tensor_mul(r_t, r_rstd, r_rstd),
                lambda e: e.tensor_mul(r_t, r_var, r_t),
                lambda e: e.tensor_scalar_add(r_t, r_t, 1e-5),
            ], sem="s_ve", waits=[("s_ac", pl.val("s_ac"))])
            ops("scalar", [
                lambda e: e.activation(r_rab, r_t, AF.Ln),
                lambda e: e.activation(r_rab, r_rab, AF.Exp, scale=-0.5),
            ], sem="s_ac", waits=[("s_ve", pl.val("s_ve"))])
            op("vector", lambda e: e.tensor_mul(r_rab, r_rstd, r_rab),
               sem="s_ve", waits=[("s_ac", pl.val("s_ac"))])
            vst = pl.val("s_ve")
            op("tensor", lambda e: e.matmul(pb[2][:, :], fs(o_ones_row, 128, 0, 1), r_mu,
                                            start=True, stop=True), waits=[("s_ve", vst)])
            op("tensor", lambda e: e.matmul(pb[3][:, :], fs(o_ones_row, 128, 0, 1), r_rstd,
                                            start=True, stop=True))
            op("tensor", lambda e: e.matmul(pb[4][:, :], fs(o_ones_row, 128, 0, 1), r_rab,
                                            start=True, stop=True), sem="s_mm")
            mmb = pl.val("s_mm")
            ops("vector", [
                lambda e: e.tensor_copy(fs(o_lbc, 512), pb[2][:, :]),
                lambda e: e.tensor_copy(fs(o_lbc + 512, 512), pb[3][:, :]),
                lambda e: e.tensor_copy(fs(o_lbc + 1024, 512), pb[4][:, :]),
            ], sem="s_ve", waits=[("s_mm", mmb)])
            for k in range(16):
                xn_dst = bcol(o_xn, k * 2048 + tb * 512, 512)
                h1_dst = bcol(o_h1, k * 2048 + tb * 512, 512)
                xb = (k % 2) * 512
                ops("vector", [
                    (lambda e, k=k: e.tensor_sub(fs(o_lxc, 512), fs(o_lx + k * 512, 512), fs(o_lbc, 512))),
                    (lambda e, xb=xb: e.tensor_mul(fs(o_lxnf + xb, 512), fs(o_lxc, 512), fs(o_lbc + 512, 512))),
                    (lambda e, d=xn_dst, xb=xb: e.tensor_copy(d, fs(o_lxnf + xb, 512))),
                    (lambda e: e.tensor_mul(fs(o_lth, 512), fs(o_lxc, 512), fs(o_lbc + 1024, 512))),
                    (lambda e, d=h1_dst, k=k: e.tensor_scalar_mul(d, fs(o_lth, 512), fs(o_rms1 + k, 1))),
                ], sem="s_ve", waits=[sc_mm[k % 2]] if sc_mm[k % 2][1] else [])
                xnf_ve = pl.val("s_ve")
                for tcq in range(4):
                    op("tensor",
                       (lambda e, tb=tb, k=k, tcq=tcq, xb=xb:
                        e.matmul(pb[5][:, (tb * 4 + tcq) * 8:(tb * 4 + tcq) * 8 + 8],
                                 fs(o_lxnf + xb + tcq * 128, 128), fs(o_wgf + k * 8, 8),
                                 start=(tb == 0 and k == 0 and tcq == 0),
                                 stop=(tb == 3 and k == 15 and tcq == 3))),
                       sem="s_mm" if tcq == 3 else None,
                       waits=[("s_ve", xnf_ve), ("s_in", glob_in)] if tcq == 0 else [])
                sc_mm[k % 2] = ("s_mm", pl.val("s_mm"))
        scev = op("vector", lambda e: e.tensor_copy(fs(o_scf, 128), pb[5][:, 0:128]),
                  sem="s_ve", waits=[("s_mm", pl.val("s_mm"))], flush=True)
        ln_done = pl.val("s_ve")
        _scev_for_bank5 = scev
        dma("sync", d3(sp_xn), s3(Abf[:, 2 * o_xn:2 * o_xn + 32768], 16),
            "s_st", waits=[("s_ve", ln_done)])

        oM = oL
        o_win = oM
        o_stf = oM + 2048
        o_stb = o_stf + 2048
        o_dtin = o_stb + 1024
        o_dtw = o_dtin + 1024
        o_sv = o_dtw + 2048
        o_sav = o_sv + 2048
        o_sl = o_sav + 2048
        assert o_sl + 2048 < ARENA_W

        dtw_t = dma("gpsimd", bcol(o_dtw, 0, 4096), dtprojT, "s_w",
                    waits=[("s_ve", ln_done)])

        bank_rel = [("s_mm", 0)] * 8
        bank_rel[5] = _scev_for_bank5
        stf_rel = [("s_st", 0)] * 4
        stb_rel = [("s_st", 0)] * 4
        win_rel = [("s_mm", 0), ("s_mm", 0)]

        def evac_m1(mc, nb, bank, kind, mmv):
            sti = (mc * 4 + nb) % 4
            if kind == "z":
                dst = bcol(o_stb, sti * 512, 512)
                ev = op("scalar", (lambda e, d=dst, b=bank: e.activation(d, pb[b][:, :], AF.Silu)),
                        sem="s_ac", waits=[("s_mm", mmv), stb_rel[sti]], flush=True)
                st = dma("sync", sp_sz[mc * 128:(mc + 1) * 128, nb * 512:(nb + 1) * 512], dst,
                         "s_st", waits=[("s_ac", ev[1])])
                stb_rel[sti] = st
                return ev
            dst = fs(o_stf + sti * 512, 512)
            ev = op("vector", (lambda e, d=dst, b=bank: e.tensor_copy(d, pb[b][:, :])),
                    sem="s_ve", waits=[("s_mm", mmv), stf_rel[sti]], flush=True)
            if kind == "xv":
                loc = mc - 32
                last = None
                for ml in range(2):
                    for r in range(4):
                        last = dma("sync",
                                   sp_x4[(2 * loc + ml) * 256 + r * 64:(2 * loc + ml) * 256 + r * 64 + 64,
                                         nb * 512:(nb + 1) * 512],
                                   A[ml * 64:(ml + 1) * 64, o_stf + sti * 512:o_stf + (sti + 1) * 512],
                                   "s_st", waits=[("s_ve", ev[1])])
                stf_rel[sti] = last
            elif kind == "bm":
                loc = mc - 40
                last = None
                for nl in range(2):
                    for q in range(2):
                        last = dma("sync",
                                   sp_b2[(2 * loc + nl) * 128 + q * 64:(2 * loc + nl) * 128 + q * 64 + 64,
                                         nb * 512:(nb + 1) * 512],
                                   A[nl * 64:(nl + 1) * 64, o_stf + sti * 512:o_stf + (sti + 1) * 512],
                                   "s_st", waits=[("s_ve", ev[1])])
                stf_rel[sti] = last
            else:
                st = dma("sync", sp_cm[(mc - 48) * 128:(mc - 47) * 128, nb * 512:(nb + 1) * 512], dst,
                         "s_st", waits=[("s_ve", ev[1])])
                stf_rel[sti] = st
            return ev

        dtin_evs = []
        for mc in range(81):
            buf = mc % 2
            wv = dma("gpsimd", bcol(o_win, buf * 2048, 2048),
                     inprojT[mc],
                     "s_w", waits=[win_rel[buf]] + ([("s_ve", ln_done)] if mc < 2 else []))
            bset = (mc % 2) * 4
            w0 = [("s_w", wv[1]), ("s_ve", ln_done)] + [bank_rel[bset + j] for j in range(4)]
            for k in range(16):
                for nb in range(4):
                    op("tensor",
                       (lambda e, k=k, nb=nb, buf=buf, bset=bset:
                        e.matmul(pb[bset + nb][:, :],
                                 bcol(o_win, buf * 2048 + k * 128, 128),
                                 bcol(o_h1, k * 2048 + nb * 512, 512),
                                 start=(k == 0), stop=(k == 15))),
                       sem="s_mm" if k == 15 else None,
                       waits=w0 if (k == 0 and nb == 0) else [])
                    if k == 15:
                        mmv = pl.val("s_mm")
                        kind = ("z" if mc < 32 else "xv" if mc < 40 else "bm" if mc < 48 else
                                "cm" if mc < 80 else "dtin")
                        if kind == "dtin":
                            ev = op("scalar",
                                    (lambda e, nb=nb, b=bset + nb:
                                     e.activation(bcol(o_dtin, nb * 512, 512), pb[b][:, :], AF.Copy)),
                                    sem="s_ac", waits=[("s_mm", mmv)], flush=True)
                            dtin_evs.append(ev)
                        else:
                            ev = evac_m1(mc, nb, bset + nb, kind, mmv)
                        bank_rel[bset + nb] = ev
            win_rel[buf] = ("s_mm", pl.val("s_mm"))

        dtin_ready = pl.val("s_ac")
        sv_rel = ("s_st", 0)
        for sc in range(32):
            bset = (sc % 2) * 4
            w0 = [("s_w", dtw_t[1]), ("s_ac", dtin_ready)] + [bank_rel[bset + j] for j in range(4)]
            for nb in range(4):
                op("tensor",
                   (lambda e, sc=sc, nb=nb, bset=bset:
                    e.matmul(pb[bset + nb][:, :], bcol(o_dtw, sc * 128, 128),
                             bcol(o_dtin, nb * 512, 512), start=True, stop=True)),
                   sem="s_mm", waits=w0 if nb == 0 else [])
                mmv = pl.val("s_mm")
                ev = op("vector",
                        (lambda e, nb=nb, b=bset + nb, sc=sc:
                         e.tensor_scalar_add(fs(o_sv + nb * 512, 512), pb[b][:, :], fs(o_dtb + sc, 1))),
                        sem="s_ve", waits=[("s_mm", mmv), sv_rel])
                bank_rel[bset + nb] = ev
            vv = pl.val("s_ve")
            av = pl.val("s_ve")
            ops("scalar", [
                lambda e: e.activation(fs(o_sav, 2048), fs(o_sv, 2048), AF.Abs),
                lambda e: e.activation(fs(o_sl, 2048), fs(o_sav, 2048), AF.Exp, scale=-1.0),
                lambda e: e.activation(fs(o_sl, 2048), fs(o_sl, 2048), AF.Ln, bias=1.0),
            ], sem="s_ac", waits=[("s_ve", av)])
            ops("vector", [
                lambda e: e.tensor_scalar_max(fs(o_sav, 2048), fs(o_sv, 2048), 0.0),
                lambda e: e.tensor_add(fs(o_sv, 2048), fs(o_sav, 2048), fs(o_sl, 2048)),
            ], sem="s_ve", waits=[("s_ac", pl.val("s_ac"))], flush=True)
            st = dma("sync", sp_dt[sc * 128:(sc + 1) * 128, :], fs(o_sv, 2048),
                     "s_st", waits=[("s_ve", pl.val("s_ve"))])
            sv_rel = st
        m1_spills = pl.val("s_st")

        phase_barrier("vector")
        phase_barrier("scalar")
        phase_barrier("sync")
        o_sc = o_h1
        o_wtk = o_sc + 256
        o_t2 = o_wtk + 128
        o_swT = o_swTp
        scv = pl.val("s_ve")
        o_t2m1 = o_t2
        o_t2m2 = o_t2 + 16
        o_t2msk = o_wtk
        o_t2s = o_sc + 128
        def stg(fl, w=()):
            for f in fl:
                op("vector", f, sem=None, waits=w)
        stg([(lambda e, tc=tc: e.tensor_reduce(fs(o_t2m1 + tc, 1), fs(o_scf + tc * 8, 8), AX.X, ALU.max))
             for tc in range(16)])
        stg([(lambda e, tc=tc: e.tensor_single_scalar(fs(o_t2msk + tc * 8, 8), fs(o_scf + tc * 8, 8),
                                                      fs(o_t2m1 + tc, 1), ALU.is_equal))
             for tc in range(16)])
        stg([(lambda e, tc=tc: e.scalar_tensor_tensor(fs(o_t2s + tc * 8, 8), fs(o_t2msk + tc * 8, 8),
                                                      -1e30, fs(o_scf + tc * 8, 8), ALU.mult, ALU.add))
             for tc in range(16)])
        stg([(lambda e, tc=tc: e.tensor_reduce(fs(o_t2m2 + tc, 1), fs(o_t2s + tc * 8, 8), AX.X, ALU.max))
             for tc in range(16)])
        stg([(lambda e, tc=tc: e.tensor_single_scalar(fs(o_t2msk + tc * 8, 8), fs(o_scf + tc * 8, 8),
                                                      fs(o_t2m2 + tc, 1), ALU.is_ge))
             for tc in range(16)])
        stg([(lambda e, tc=tc: e.tensor_single_scalar(fs(o_t2s + tc * 8, 8), fs(o_scf + tc * 8, 8),
                                                      fs(o_t2m1 + tc, 1), ALU.subtract))
             for tc in range(15)])
        op("vector", (lambda e: e.tensor_single_scalar(fs(o_t2s + 15 * 8, 8), fs(o_scf + 15 * 8, 8),
                                                       fs(o_t2m1 + 15, 1), ALU.subtract)))
        op("vector", lambda e: e.tensor_copy(fs(o_t2m2, 1), fs(o_t2s, 1)))
        op("vector", lambda e: e.tensor_copy(fs(o_t2m2 + 1, 1), fs(o_t2s + 8, 1)), sem="s_ve")
        op("scalar", (lambda e: e.activation(fs(o_t2s, 128), fs(o_t2s, 128), AF.Exp)),
           sem="s_ac", waits=[("s_ve", pl.val("s_ve"))])
        acx = pl.val("s_ac")
        stg([(lambda e, tc=tc: e.tensor_mul(fs(o_t2s + tc * 8, 8), fs(o_t2s + tc * 8, 8),
                                            fs(o_t2msk + tc * 8, 8)))
             for tc in range(16)], w=[("s_ac", acx)])
        stg([(lambda e, tc=tc: e.tensor_reduce(fs(o_t2m1 + tc, 1), fs(o_t2s + tc * 8, 8), AX.X, ALU.add))
             for tc in range(16)])
        stg([(lambda e, tc=tc: e.reciprocal(fs(o_t2m2 + tc, 1), fs(o_t2m1 + tc, 1)))
             for tc in range(16)])
        stg([(lambda e, tc=tc: e.tensor_scalar_mul(fs(o_wtk + tc * 8, 8), fs(o_t2s + tc * 8, 8),
                                                   fs(o_t2m2 + tc, 1)))
             for tc in range(16)])
        op("vector", lambda e: e.nop(), sem="s_ve")
        for tc in range(16):
            b = tc % 8
            op("tensor", (lambda e, tc=tc, b=b:
                          e.transpose(pb[b][0:8, 0:128], fs(o_wtk + tc * 8, 8), fs(o_ident, 128))),
               sem="s_mm", waits=[("s_ve", pl.val("s_ve")), bank_rel[b]])
            ev = op("vector", (lambda e, tc=tc, b=b:
                               e.tensor_copy(fs(o_swT + tc * 128, 128, 0, 8), pb[b][0:8, 0:128])),
                    sem="s_ve", waits=[("s_mm", pl.val("s_mm"))])
            bank_rel[b] = ev
        w_ready = pl.val("s_ve")
        if debug:
            dma("sync", dbg_w, fs(o_wtk, 128), "s_st", waits=[("s_ve", w_ready)])
            dma("sync", dbg_swT, fs(o_swT, 2048, 0, 8), "s_st", waits=[("s_ve", w_ready)])
            dma("sync", dbg_sc[:, 0:128], fs(o_scf, 128), "s_st", waits=[("s_ve", w_ready)])

        oM0 = o_xn + 2048 + 0
        oM0 = G0 + 2048 + 0
        oM0 = 4224 if G0 == 2176 else G0 + 2048
        o_w3 = oM0
        o_w12 = o_w3 + 11264
        o_xnh = o_w12 + 4096
        o_she = o_xnh + 8192
        o_swb = o_she + 5632
        o_sg = o_swb + 1024
        o_hf = o_sg + 1024
        o_ypi = o_hf + 512
        o_yps = o_ypi + 512
        oM_end = o_yps + 512
        assert oM_end == 36992 or True
        oP0 = oM_end
        o_pB2 = oP0
        o_pCm = o_pB2 + 4096
        o_pdt = o_pCm + 8192
        o_px4 = o_pdt + 512
        o_py = o_px4 + 512
        o_pdA = o_py + 512
        o_pdBx = o_pdA + 1024
        o_pshh = o_pdBx + 256
        o_hprev = o_pshh + 256
        o_dtend = o_hprev + 512
        o_dte2 = o_dtend + 32
        o_pend = o_pCm + 4096
        assert o_dte2 + 32 <= ARENA_W
        o2_cm = o_pB2
        o2_dt = o_pdt
        o2_y = o_px4
        o2_sz = o_py
        o2_P = o_pdA
        o2_tc = o_pdBx
        o2_ygt = o_pshh
        o_dtc = o_dtend
        o_hin = o_hprev
        o_scc = o_pCm
        o_ct1 = o_pB2
        o_ct2 = o_pB2 + 512

        phase_barrier("sync")
        phase_barrier("vector")
        phase_barrier("scalar")
        phase_barrier("gpsimd")

        ymoe_st = {}
        w12_rel = [("s_mm", 0)] * 4
        w3_rel = ("s_mm", 0)
        xnh_rel = ("s_mm", 0)
        ypi_rel = ("s_ve", 0)
        yps_rel = ("s_st", 0)

        def emit_moe_unit(e_, th):
            nonlocal w3_rel, xnh_rel, ypi_rel, yps_rel
            if th == 0:
                w3v = None
                for k in range(11):
                    w3v = dma("gpsimd", bcol(o_w3, k * 2048, 2048),
                              w3T[e_][k * 128:(k + 1) * 128, :], "s_w",
                              waits=[w3_rel] if k == 0 else [])
                self_w3 = w3v
            else:
                self_w3 = ("s_w", pl.val("s_w"))
            xv = dma("sync", s3(bcol(o_xnh, 0, 16384), 16),
                     d3(sp_xn[:, th * 1024:(th + 1) * 1024]), "s_ld", waits=[xnh_rel])
            for j in range(2):
                nb = th * 2 + j
                b = 6 + j
                op("tensor",
                   (lambda e, e_=e_, nb=nb, b=b:
                    e.matmul(pb[b][:, :], fs(o_onehot + e_ * 128, 128, 0, 8),
                             fs(o_swTp + nb * 512, 512, 0, 8), start=True, stop=True)),
                   sem="s_mm", waits=[("s_ve", w_ready), bank_rel[b]])
                ev = op("vector",
                        (lambda e, j=j, b=b: e.tensor_copy(fs(o_swb + j * 512, 512), pb[b][:, :])),
                        sem="s_ve", waits=[("s_mm", pl.val("s_mm"))], flush=(j == 1))
                bank_rel[b] = ev
            wb_ready = pl.val("s_ve")
            mmv2 = 0
            for mc in range(11):
                buf = mc % 2
                w1v = dma("gpsimd", bcol(o_w12, buf * 4096, 4096),
                          w12T[e_][mc],
                          "s_w", waits=[w12_rel[buf], w12_rel[2 + buf]])
                wA = [("s_w", w1v[1]), ("s_ld", xv[1]), bank_rel[0], bank_rel[1]]
                for k in range(16):
                    for j in range(2):
                        op("tensor",
                           (lambda e, k=k, j=j, buf=buf:
                            e.matmul(pb[j][:, :], bcol(o_w12, buf * 4096 + k * 128, 128),
                                     bcol(o_xnh, k * 1024 + j * 512, 512),
                                     start=(k == 0), stop=(k == 15))),
                           sem="s_mm" if k == 15 else None,
                           waits=wA if (k == 0 and j == 0) else [])
                mmv1 = pl.val("s_mm")
                wB = [bank_rel[2], bank_rel[3]]
                for k in range(16):
                    for j in range(2):
                        op("tensor",
                           (lambda e, k=k, j=j, buf=buf:
                            e.matmul(pb[2 + j][:, :], bcol(o_w12, buf * 4096 + 2048 + k * 128, 128),
                                     bcol(o_xnh, k * 1024 + j * 512, 512),
                                     start=(k == 0), stop=(k == 15))),
                           sem="s_mm" if k == 15 else None,
                           waits=wB if (k == 0 and j == 0) else [])
                mmv2 = pl.val("s_mm")
                for j in range(2):
                    ev1 = op("scalar",
                             (lambda e, j=j: e.activation(fs(o_sg + j * 512, 512), pb[j][:, :], AF.Silu)),
                             sem="s_ac", waits=[("s_mm", mmv1)], flush=True)
                    bank_rel[j] = ev1
                acv = pl.val("s_ac")
                for j in range(2):
                    ev2 = ops("vector", [
                        (lambda e, j=j: e.tensor_mul(fs(o_hf, 512), pb[2 + j][:, :], fs(o_sg + j * 512, 512))),
                        (lambda e, j=j, mc=mc: e.tensor_mul(bcol(o_she, mc * 1024 + j * 512, 512),
                                                            fs(o_hf, 512), fs(o_swb + j * 512, 512))),
                    ], sem="s_ve", waits=[("s_mm", mmv2), ("s_ac", acv), ("s_ve", wb_ready)],
                        flush=(mc == 10))
                    bank_rel[2 + j] = ev2
                w12_rel[buf] = ("s_mm", mmv1)
                w12_rel[2 + buf] = ("s_mm", mmv2)
            xnh_rel = ("s_mm", mmv2)
            she_ready = pl.val("s_ve")
            for omc in range(16):
                for j in range(2):
                    b = 4 + (omc % 2) * 2 + j if False else 4 + ((omc * 2 + j) % 4)
                    nb = th * 2 + j
                    wC = [("s_w", self_w3[1]), ("s_ve", she_ready), bank_rel[b]]
                    for k in range(11):
                        op("tensor",
                           (lambda e, k=k, j=j, omc=omc, b=b:
                            e.matmul(pb[b][:, :], bcol(o_w3, k * 2048 + omc * 128, 128),
                                     bcol(o_she, k * 1024 + j * 512, 512),
                                     start=(k == 0), stop=(k == 10))),
                           sem="s_mm" if k == 10 else None,
                           waits=wC if k == 0 else [])
                    mmv3 = pl.val("s_mm")
                    if e_ > 0:
                        ldv = dma("sync", fs(o_ypi, 512),
                                  sp_ymoe[omc * 128:(omc + 1) * 128, nb * 512:(nb + 1) * 512],
                                  "s_ld", waits=[ymoe_st.get((omc, nb), ("s_st", 0)), ypi_rel])
                        ev = op("vector",
                                (lambda e, b=b: e.tensor_add(fs(o_yps, 512), pb[b][:, :], fs(o_ypi, 512))),
                                sem="s_ve",
                                waits=[("s_mm", mmv3), ("s_ld", ldv[1]), yps_rel], flush=True)
                        ypi_rel = ev
                    else:
                        ev = op("vector",
                                (lambda e, b=b: e.tensor_copy(fs(o_yps, 512), pb[b][:, :])),
                                sem="s_ve", waits=[("s_mm", mmv3), yps_rel], flush=True)
                    st = dma("sync", sp_ymoe[omc * 128:(omc + 1) * 128, nb * 512:(nb + 1) * 512],
                             fs(o_yps, 512), "s_st", waits=[("s_ve", ev[1])])
                    ymoe_st[(omc, nb)] = st
                    yps_rel = st
                    bank_rel[b] = ev
            if th == 1:
                w3_rel = ("s_mm", pl.val("s_mm"))

        op("vector", lambda e: e.memset(fs(o_hprev, 512), 0.0))
        op("vector", lambda e: e.memset(fs(o_dtend, 32), 0.0), sem="s_ve")
        pB2_rel = ("s_ve", 0)
        pCm_rel = ("s_ve", 0)
        pdt_rel = [("s_ac", 0)] * 2
        px4_rel = [("s_ve", 0)] * 2
        py_rel = [("s_st", 0)] * 2
        pdA_rel = [("s_ve", 0)] * 4

        def emit_p1_tile(ti):
            nonlocal pB2_rel, pCm_rel
            tb0 = ti * TT
            b2v = dma("sync", s3(fs(o_pB2, 4096), 16), d3(sp_b2[:, tb0:tb0 + TT]),
                      "s_ld", waits=[("s_st", m1_spills), pB2_rel])
            cmv = dma("sync", s3(fs(o_pCm, 8192), 32), d3(sp_cm[:, tb0:tb0 + TT]),
                      "s_ld", waits=[pCm_rel])
            for cb in range(32):
                pf = cb % 2
                dtv = dma("sync", fs(o_pdt + pf * 256, TT),
                          sp_dt[cb * 128:(cb + 1) * 128, tb0:tb0 + TT], "s_ld",
                          waits=[pdt_rel[pf]])
                x4v = dma("sync", fs(o_px4 + pf * 256, TT),
                          sp_x4[cb * 128:(cb + 1) * 128, tb0:tb0 + TT], "s_ld",
                          waits=[px4_rel[pf]])
                ops("vector", [
                    (lambda e, cb=cb, pf=pf: e.tensor_scalar_mul(
                        fs(o_py + pf * 256, TT), fs(o_px4 + pf * 256, TT), fs(o_dsk + cb, 1))),
                    (lambda e, cb=cb, pf=pf: e.tensor_mul(
                        fs(o_px4 + pf * 256, TT), fs(o_pdt + pf * 256, TT), fs(o_px4 + pf * 256, TT))),
                    (lambda e, cb=cb, pf=pf: e.tensor_reduce(
                        fs(o_dte2 + cb, 1), fs(o_pdt + pf * 256, TT), AX.X, ALU.add)),
                ], sem="s_ve", waits=[("s_ld", dtv[1]), ("s_ld", x4v[1]), py_rel[pf]])
                dtx_mark = pl.val("s_ve")
                lastv = None
                for n_ in range(16):
                    ab = n_ % 4
                    av = op("scalar",
                            (lambda e, cb=cb, n_=n_, pf=pf, ab=ab:
                             e.activation(fs(o_pdA + ab * 256, TT), fs(o_pdt + pf * 256, TT),
                                          AF.Exp, scale=fs(o_negA + cb * 16 + n_, 1))),
                            sem="s_ac",
                            waits=([("s_ve", dtx_mark)] if n_ < 4 else []) + [pdA_rel[ab]])
                    lastv = ops("vector", [
                        (lambda e, n_=n_, pf=pf: e.tensor_mul(
                            fs(o_pdBx, TT), fs(o_px4 + pf * 256, TT),
                            fs(o_pB2 + n_ * 256, TT))),
                        (lambda e, cb=cb, n_=n_, ab=ab: e.tensor_tensor_scan(
                            fs(o_pshh, TT), fs(o_pdA + ab * 256, TT), fs(o_pdBx, TT),
                            fs(o_hprev + cb * 16 + n_, 1), ALU.mult, ALU.add)),
                        (lambda e, cb=cb, n_=n_: e.tensor_mul(
                            fs(o_pdBx, TT), fs(o_pshh, TT),
                            fs(o_pCm + (n_ * 2 + cb % 2) * 256, TT))),
                        (lambda e, cb=cb, pf=pf: e.tensor_add(
                            fs(o_py + pf * 256, TT), fs(o_py + pf * 256, TT), fs(o_pdBx, TT))),
                        (lambda e, cb=cb, n_=n_: e.tensor_copy(
                            fs(o_hprev + cb * 16 + n_, 1), A[:, o_pshh + TT - 1:o_pshh + TT])),
                    ], sem="s_ve",
                        waits=[("s_ac", av[1]), ("s_ld", b2v[1]), ("s_ld", cv2 := cmv[1])],
                        flush=(n_ == 15))
                    pdA_rel[ab] = lastv
                st = dma("sync", sp_yloc[cb * 128:(cb + 1) * 128, tb0:tb0 + TT],
                         fs(o_py + pf * 256, TT), "s_st", waits=[("s_ve", lastv[1])])
                py_rel[pf] = st
                pdt_rel[pf] = ("s_ac", pl.val("s_ac"))
                px4_rel[pf] = lastv
            op("vector", lambda e: e.tensor_add(fs(o_dtend, 16), fs(o_dtend, 16), fs(o_dte2, 16)))
            op("vector", lambda e: e.tensor_copy(fs(o_pshh, 1), fs(o_dte2, 1)))
            op("vector", lambda e: e.tensor_add(fs(o_dtend + 16, 16), fs(o_dtend + 16, 16),
                                                fs(o_dte2 + 16, 16)), sem="s_ve")
            pB2_rel = ("s_ve", pl.val("s_ve"))
            pCm_rel = ("s_ve", pl.val("s_ve"))

        def emit_ag():
            for cb in range(32):
                for n_ in range(16):
                    op("vector",
                       (lambda e, cb=cb, n_=n_: e.tensor_mul(
                           fs(o_pend + cb * 16 + n_, 1), fs(o_dtend + cb, 1), fs(o_negA + cb * 16 + n_, 1))),
                       sem="s_ve" if (cb == 31 and n_ == 15) else None)
            op("scalar", lambda e: e.activation(fs(o_pend, 512), fs(o_pend, 512), AF.Exp),
               sem="s_ac", waits=[("s_ve", pl.val("s_ve"))], flush=True)
            pend_done = pl.val("s_ac")
            dma("sync", cc_in.rearrange("a (p x) -> (a p) x", p=128)[:, 0:512],
                fs(o_pend, 512), "s_st", waits=[("s_ac", pend_done)])
            dma("sync", cc_in.rearrange("a (p x) -> (a p) x", p=128)[:, 512:1024],
                fs(o_hprev, 512), "s_st", waits=[("s_ve", pl.val("s_ve"))])
            op("gpsimd", (lambda e: e.collective_compute(
                "AllGather", ALU.bypass, replica_groups=[[0, 1, 2, 3], [4, 5, 6, 7]],
                ins=[cc_in.opt()], outs=[cc_out.opt()])),
                sem="s_gp", waits=[("s_st", pl.val("s_st"))])
            cc_done = pl.val("s_gp")
            ldcc = []
            for j in range(4):
                ldcc.append(dma("sync", fs(o_scc + j * 1024, 1024),
                                cc_out[j:j + 1, :].rearrange("a (p x) -> (a p) x", p=128),
                                "s_ld", waits=[("s_gp", cc_done)]))
            PAIR_IDX = {(0, 1): 0, (0, 2): 1, (0, 3): 2, (1, 2): 3, (1, 3): 4, (2, 3): 5}
            op("vector", lambda e: e.memset(fs(o_hin, 512), 0.0), sem="s_ve",
               waits=[("s_ld", ldcc[-1][1])])
            for j in range(3):
                fl = [(lambda e, j=j: e.tensor_copy(fs(o_ct1, 512), fs(o_scc + j * 1024 + 512, 512)))]
                for i_ in range(j + 1, 4):
                    pi = PAIR_IDX[(j, i_)]
                    fl.append((lambda e, i_=i_, pi=pi: e.tensor_scalar(
                        fs(o_ct2, 512), fs(o_scc + i_ * 1024, 512),
                        fs(o_umask + 3 + pi, 1), fs(o_umask + 9 + pi, 1), ALU.mult, ALU.add)))
                    fl.append((lambda e: e.tensor_mul(fs(o_ct1, 512), fs(o_ct1, 512), fs(o_ct2, 512))))
                fl.append((lambda e, j=j: e.scalar_tensor_tensor(
                    fs(o_hin, 512), fs(o_ct1, 512), fs(o_umask + j, 1), fs(o_hin, 512),
                    ALU.mult, ALU.add)))
                ops("vector", fl, sem="s_ve", flush=(j == 2))
            return pl.val("s_ve")

        p2cm_rel = ("s_ve", 0)
        p2dt_rel = [("s_ac", 0)] * 2
        p2y_rel = [("s_ve", 0)] * 2
        p2sz_rel = [("s_ve", 0)] * 2
        p2P_rel = [("s_ve", 0)] * 4
        p2ygt_rel = [("s_st", 0)] * 2

        def emit_p2_first():
            op("vector", lambda e: e.memset(fs(o_dtc, 32), 0.0), sem="s_ve")

        def emit_p2_tile(ti, hin_done):
            nonlocal p2cm_rel
            tb0 = ti * TT
            cmv = dma("sync", s3(fs(o2_cm, 8192), 32), d3(sp_cm[:, tb0:tb0 + TT]),
                      "s_ld", waits=[("s_ve", hin_done), p2cm_rel])
            for cb in range(32):
                pf = cb % 2
                dtv = dma("sync", fs(o2_dt + pf * 256, TT),
                          sp_dt[cb * 128:(cb + 1) * 128, tb0:tb0 + TT], "s_ld",
                          waits=[p2dt_rel[pf]])
                yv = dma("sync", fs(o2_y + pf * 256, TT),
                         sp_yloc[cb * 128:(cb + 1) * 128, tb0:tb0 + TT], "s_ld",
                         waits=[p2y_rel[pf]])
                szv = dma("sync", bcol(o2_sz, pf * 256, TT),
                          sp_sz[cb * 128:(cb + 1) * 128, tb0:tb0 + TT], "s_ld",
                          waits=[p2sz_rel[pf]])
                op("vector",
                   (lambda e, cb=cb, pf=pf: e.tensor_tensor_scan(
                       fs(o2_dt + pf * 256, TT), fs(o2_dt + pf * 256, TT), fs(o2_dt + pf * 256, TT),
                       fs(o_dtc + cb, 1), ALU.add, ALU.bypass)),
                   sem="s_ve", waits=[("s_ld", dtv[1]), ("s_ve", hin_done)])
                cums = pl.val("s_ve")
                lastv = None
                for n_ in range(16):
                    ab = n_ % 4
                    av = op("scalar",
                            (lambda e, cb=cb, n_=n_, pf=pf, ab=ab:
                             e.activation(fs(o2_P + ab * 256, TT), fs(o2_dt + pf * 256, TT),
                                          AF.Exp, scale=fs(o_negA + cb * 16 + n_, 1))),
                            sem="s_ac",
                            waits=([("s_ve", cums)] if n_ < 4 else []) + [p2P_rel[ab]])
                    lastv = ops("vector", [
                        (lambda e, cb=cb, n_=n_, ab=ab: e.scalar_tensor_tensor(
                            fs(o2_tc, TT), fs(o2_P + ab * 256, TT),
                            fs(o_hin + cb * 16 + n_, 1),
                            fs(o2_cm + (n_ * 2 + cb % 2) * 256, TT),
                            ALU.mult, ALU.mult)),
                        (lambda e, pf=pf: e.tensor_add(
                            fs(o2_y + pf * 256, TT), fs(o2_y + pf * 256, TT), fs(o2_tc, TT))),
                    ], sem="s_ve",
                        waits=[("s_ac", av[1]), ("s_ld", cmv[1]), ("s_ld", yv[1])])
                    p2P_rel[ab] = lastv
                gv = ops("vector", [
                    (lambda e, pf=pf: e.tensor_mul(bcol(o2_ygt, pf * 256, TT),
                                                   fs(o2_y + pf * 256, TT), bcol(o2_sz, pf * 256, TT))),
                    (lambda e, cb=cb, pf=pf: e.tensor_copy(
                        fs(o_dtc + cb, 1), A[:, o2_dt + pf * 256 + TT - 1:o2_dt + pf * 256 + TT])),
                ], sem="s_ve", waits=[("s_ld", szv[1]), p2ygt_rel[pf]], flush=True)
                st = dma("sync", sp_yg[cb * 128:(cb + 1) * 128, tb0:tb0 + TT],
                         bcol(o2_ygt, pf * 256, TT), "s_st", waits=[("s_ve", gv[1])])
                p2ygt_rel[pf] = st
                p2dt_rel[pf] = ("s_ve", gv[1])
                p2y_rel[pf] = ("s_ve", gv[1])
                p2sz_rel[pf] = ("s_ve", gv[1])
            p2cm_rel = ("s_ve", pl.val("s_ve"))

        units = [(e_, th) for e_ in range(8) for th in range(2)]
        for ti in range(NTILE):
            emit_p1_tile(ti)
            emit_moe_unit(*units[ti])
        hin_done = emit_ag()
        _stnow = pl.val("s_st")
        p2dt_rel = [("s_ve", hin_done)] * 2
        p2y_rel = [("s_ve", hin_done)] * 2
        p2sz_rel = [("s_st", _stnow)] * 2
        p2P_rel = [("s_ve", hin_done)] * 4
        p2ygt_rel = [("s_st", _stnow)] * 2
        emit_p2_first()
        for ti in range(NTILE):
            emit_p2_tile(ti, hin_done)
            emit_moe_unit(*units[NTILE + ti])
        p2_st = pl.val("s_st")
        moe_done_st = pl.val("s_st")

        o_ygr = G0
        o_wop = o_ygr + 32768
        o_xns = o_wop + 4096
        o_h2s = o_xns + 2048
        assert o_h2s + 1024 < ARENA_W

        phase_barrier("sync")
        phase_barrier("gpsimd")
        ygrv = dma("sync", s3(bcol(o_ygr, 0, 65536), 32),
                   d3(sp_yg), "s_ld",
                   waits=[("s_st", p2_st)])
        wop_rel = [("s_mm", 0)] * 2
        xns_rel = [("s_mm", 0)] * 2
        h2s_rel = [("s_st", 0)] * 2
        for omc in range(16):
            buf = omc % 2
            wv = dma("gpsimd", bcol(o_wop, buf * 4096, 4096),
                     outprojT[omc],
                     "s_w", waits=[wop_rel[buf]])
            xv = dma("sync", bcol(o_xns, buf * 2048, 2048),
                     sp_xn[omc * 128:(omc + 1) * 128, :], "s_ld", waits=[xns_rel[buf]])
            bset = (omc % 2) * 4
            w0 = [("s_w", wv[1]), ("s_ld", ygrv[1])] + [bank_rel[bset + j] for j in range(4)]
            for cb in range(32):
                for nb in range(4):
                    op("tensor",
                       (lambda e, cb=cb, nb=nb, buf=buf, bset=bset:
                        e.matmul(pb[bset + nb][:, :], bcol(o_wop, buf * 4096 + cb * 128, 128),
                                 bcol(o_ygr, cb * 2048 + nb * 512, 512),
                                 start=(cb == 0), stop=(cb == 31))),
                       sem="s_mm" if cb == 31 else None,
                       waits=w0 if (cb == 0 and nb == 0) else [])
            mmv = pl.val("s_mm")
            for nb in range(4):
                sti = nb % 2
                ev = op("vector",
                        (lambda e, nb=nb, b=bset + nb, buf=buf, sti=sti:
                         e.tensor_add(fs(o_h2s + sti * 512, 512), pb[b][:, :],
                                      bcol(o_xns, buf * 2048 + nb * 512, 512))),
                        sem="s_ve", waits=[("s_mm", mmv), ("s_ld", xv[1]), h2s_rel[sti]], flush=True)
                st = dma("sync", sp_h2[omc * 128:(omc + 1) * 128, nb * 512:(nb + 1) * 512],
                         fs(o_h2s + sti * 512, 512), "s_st", waits=[("s_ve", ev[1])])
                h2s_rel[sti] = st
                bank_rel[bset + nb] = ev
            wop_rel[buf] = ("s_mm", mmv)
            xns_rel[buf] = ("s_mm", mmv)
        h2_st = pl.val("s_st")

        o_h2b = G0
        _ = phase_barrier("sync")
        o_sq2 = o_h2b + 8192
        o_hrow = o_sq2 + 1024
        o_hbc = o_hrow + 1024
        o_hnb = o_hbc + 512
        h2b_rel = ("s_ve", 0)
        for nb in range(4):
            ldv = dma("sync", s3(fs(o_h2b, 8192), 16),
                      d3(sp_h2[:, nb * 512:(nb + 1) * 512]),
                      "s_ld", waits=[("s_st", h2_st), h2b_rel])
            for k in range(16):
                sq = k % 2
                aq = op("scalar",
                        (lambda e, k=k, sq=sq: e.activation(fs(o_sq2 + sq * 512, 512),
                                                            fs(o_h2b + k * 512, 512), AF.Square)),
                        sem="s_ac",
                        waits=[("s_ld", ldv[1])] + ([("s_mm", pl.val("s_mm"))] if k >= 2 else []))
                op("tensor",
                   (lambda e, k=k, sq=sq: e.matmul(pb[0][0:1, :], fs(o_ones_col, 1),
                                                   fs(o_sq2 + sq * 512, 512),
                                                   start=(k == 0), stop=(k == 15))),
                   sem="s_mm", waits=[("s_ac", aq[1])])
            mmv = pl.val("s_mm")
            ops("vector", [
                lambda e: e.tensor_scalar_mul(fs(o_hrow, 512, 0, 1), pb[0][0:1, :], 1.0 / 2048.0),
                lambda e: e.tensor_scalar_add(fs(o_hrow, 512, 0, 1), fs(o_hrow, 512, 0, 1), 1e-5),
            ], sem="s_ve", waits=[("s_mm", mmv)])
            ops("scalar", [
                lambda e: e.activation(fs(o_hrow + 512, 512, 0, 1), fs(o_hrow, 512, 0, 1), AF.Ln),
                lambda e: e.activation(fs(o_hrow + 512, 512, 0, 1), fs(o_hrow + 512, 512, 0, 1),
                                       AF.Exp, scale=-0.5),
            ], sem="s_ac", waits=[("s_ve", pl.val("s_ve"))])
            op("tensor", lambda e: e.matmul(pb[1][:, :], fs(o_ones_row, 128, 0, 1),
                                            fs(o_hrow + 512, 512, 0, 1), start=True, stop=True),
               sem="s_mm", waits=[("s_ac", pl.val("s_ac"))])
            mmv2 = pl.val("s_mm")
            op("vector", lambda e: e.tensor_copy(fs(o_hbc, 512), pb[1][:, :]),
               sem="s_ve", waits=[("s_mm", mmv2)])
            bcv = pl.val("s_ve")
            for k in range(16):
                ops("vector", [
                    (lambda e, k=k: e.tensor_mul(fs(o_h2b + k * 512, 512),
                                                 fs(o_h2b + k * 512, 512), fs(o_hbc, 512))),
                    (lambda e, k=k: e.tensor_scalar_mul(bcol(o_hnb, k * 512, 512),
                                                        fs(o_h2b + k * 512, 512), fs(o_rms2 + k, 1))),
                ], sem="s_ve" if k == 15 else None, waits=[("s_ve", bcv)] if k == 0 else [],
                    flush=(k == 15))
            hn_done = pl.val("s_ve")
            st = dma("sync", d3(sp_hn[:, nb * 512:(nb + 1) * 512]),
                     s3(bcol(o_hnb, 0, 8192), 16), "s_st", waits=[("s_ve", hn_done)])
            h2b_rel = ("s_ve", hn_done)
        hn_st = pl.val("s_st")

        o_shn = G0
        o_smq = o_shn + 16384
        o_wg1 = o_smq + 16384
        o_sg3 = o_wg1 + 4096
        assert o_sg3 + 2048 <= ARENA_W

        phase_barrier("sync")
        phase_barrier("gpsimd")
        hnv = dma("sync", s3(bcol(o_shn, 0, 32768), 16),
                  d3(sp_hn), "s_ld",
                  waits=[("s_st", hn_st)])
        wg_rel = [("s_mm", 0)] * 4
        smq_rel = ("s_st", 0)
        sg3_rel = [("s_ve", 0)] * 4
        for q in range(4):
            for imc in range(16):
                gmc = q * 16 + imc
                buf = imc % 2
                g1 = dma("gpsimd", bcol(o_wg1, buf * 4096, 4096),
                         gupT[gmc],
                         "s_w", waits=[wg_rel[buf], wg_rel[2 + buf]])
                wA = [("s_w", g1[1]), ("s_ld", hnv[1])] + [bank_rel[j] for j in range(4)] + \
                     ([smq_rel] if imc == 0 else [])
                for k in range(16):
                    for nb in range(4):
                        op("tensor",
                           (lambda e, k=k, nb=nb, buf=buf:
                            e.matmul(pb[nb][:, :], bcol(o_wg1, buf * 4096 + k * 128, 128),
                                     bcol(o_shn, k * 2048 + nb * 512, 512),
                                     start=(k == 0), stop=(k == 15))),
                           sem="s_mm" if k == 15 else None,
                           waits=wA if (k == 0 and nb == 0) else [])
                mmv1 = pl.val("s_mm")
                wB = [bank_rel[4 + j] for j in range(4)]
                for k in range(16):
                    for nb in range(4):
                        op("tensor",
                           (lambda e, k=k, nb=nb, buf=buf:
                            e.matmul(pb[4 + nb][:, :], bcol(o_wg1, buf * 4096 + 2048 + k * 128, 128),
                                     bcol(o_shn, k * 2048 + nb * 512, 512),
                                     start=(k == 0), stop=(k == 15))),
                           sem="s_mm" if k == 15 else None,
                           waits=wB if (k == 0 and nb == 0) else [])
                mmv2 = pl.val("s_mm")
                for nb in range(4):
                    ev1 = op("scalar",
                             (lambda e, nb=nb: e.activation(fs(o_sg3 + nb * 512, 512), pb[nb][:, :], AF.Silu)),
                             sem="s_ac", waits=[("s_mm", mmv1), sg3_rel[nb]], flush=True)
                    bank_rel[nb] = ev1
                acv = pl.val("s_ac")
                for nb in range(4):
                    ev2 = op("vector",
                             (lambda e, nb=nb, imc=imc: e.tensor_mul(
                                 bcol(o_smq, imc * 2048 + nb * 512, 512),
                                 pb[4 + nb][:, :], fs(o_sg3 + nb * 512, 512))),
                             sem="s_ve", waits=[("s_mm", mmv2), ("s_ac", acv)],
                             flush=(imc == 15))
                    bank_rel[4 + nb] = ev2
                    sg3_rel[nb] = ev2
                wg_rel[buf] = ("s_mm", mmv1)
                wg_rel[2 + buf] = ("s_mm", mmv2)
            mq_done = pl.val("s_ve")
            st = dma("sync", d3(sp_m[q * 2048:(q + 1) * 2048, :]),
                     s3(bcol(o_smq, 0, 32768), 16), "s_st", waits=[("s_ve", mq_done)])
            smq_rel = st
        m_st = pl.val("s_st")

        o_out = G0
        o_smd = o_out + 32768
        o_wdn = o_smd + 16384
        o_fst = o_smd
        assert o_wdn + 1024 <= ARENA_W

        phase_barrier("sync")
        phase_barrier("gpsimd")
        smd_rel = ("s_mm", 0)
        wdn_rel = [("s_mm", 0)] * 2
        out_ready = [[None] * 4 for _ in range(16)]
        for q in range(4):
            mdv = dma("sync", s3(bcol(o_smd, 0, 32768), 16),
                      d3(sp_m[q * 2048:(q + 1) * 2048, :]),
                      "s_ld", waits=[("s_st", m_st), smd_rel])
            for omc in range(16):
                buf = 0
                wv = dma("gpsimd", bcol(o_wdn, 0, 2048),
                         downT[q][omc],
                         "s_w", waits=[wdn_rel[0]])
                bset = (omc % 2) * 4
                w0 = [("s_w", wv[1]), ("s_ld", mdv[1])] + [bank_rel[bset + j] for j in range(4)]
                for k in range(16):
                    for nb in range(4):
                        op("tensor",
                           (lambda e, k=k, nb=nb, buf=buf, bset=bset:
                            e.matmul(pb[bset + nb][:, :], bcol(o_wdn, buf * 2048 + k * 128, 128),
                                     bcol(o_smd, k * 2048 + nb * 512, 512),
                                     start=(k == 0), stop=(k == 15))),
                           sem="s_mm" if k == 15 else None,
                           waits=w0 if (k == 0 and nb == 0) else [])
                mmv = pl.val("s_mm")
                for nb in range(4):
                    if q == 0:
                        ev = op("vector",
                                (lambda e, omc=omc, nb=nb, b=bset + nb:
                                 e.tensor_copy(fs(o_out + omc * 2048 + nb * 512, 512), pb[b][:, :])),
                                sem="s_ve", waits=[("s_mm", mmv)])
                    else:
                        ev = op("vector",
                                (lambda e, omc=omc, nb=nb, b=bset + nb:
                                 e.tensor_add(fs(o_out + omc * 2048 + nb * 512, 512),
                                              fs(o_out + omc * 2048 + nb * 512, 512), pb[b][:, :])),
                                sem="s_ve", waits=[("s_mm", mmv)])
                    bank_rel[bset + nb] = ev
                    out_ready[omc][nb] = ev
                wdn_rel[0] = ("s_mm", mmv)
            smd_rel = ("s_mm", pl.val("s_mm"))
        mlp_done = pl.val("s_ve")

        fst_rel = ("s_ve", 0)
        for nb in range(4):
            for src_i, src in enumerate((xT, sp_h2, sp_ymoe)):
                ldv = dma("sync", s3(fs(o_fst, 8192), 16),
                          d3(src[:, nb * 512:(nb + 1) * 512]),
                          "s_ld", waits=[fst_rel] + ([("s_ve", mlp_done)] if nb == 0 and src_i == 0 else []) +
                          ([("s_st", moe_done_st)] if src_i == 2 and nb == 0 else []))
                for k in range(16):
                    op("vector",
                       (lambda e, k=k, nb=nb: e.tensor_add(
                           fs(o_out + k * 2048 + nb * 512, 512),
                           fs(o_out + k * 2048 + nb * 512, 512), fs(o_fst + k * 512, 512))),
                       sem="s_ve" if k == 15 else None,
                       waits=[("s_ld", ldv[1])] if k == 0 else [],
                       flush=(k == 15))
                fst_rel = ("s_ve", pl.val("s_ve"))
        final_done = pl.val("s_ve")
        dma("sync", d3(outT), s3(fs(o_out, 32768), 16),
            "s_out", waits=[("s_ve", final_done)])
        out_t = pl.val("s_out")

        def fin(ctx):
            ctx.wait("s_out", out_t)
        pl.step("sync", fin)

        @block.tensor
        def _(eng):
            ctx = EmitCtx(eng, SEMS)
            for fn in pl.steps["tensor"]:
                fn(ctx)

        @block.vector
        def _(eng):
            ctx = EmitCtx(eng, SEMS)
            for fn in pl.steps["vector"]:
                fn(ctx)

        @block.scalar
        def _(eng):
            ctx = EmitCtx(eng, SEMS)
            for fn in pl.steps["scalar"]:
                fn(ctx)

        @block.gpsimd
        def _(eng):
            ctx = EmitCtx(eng, SEMS)
            for fn in pl.steps["gpsimd"]:
                fn(ctx)

        @block.sync
        def _(eng):
            ctx = EmitCtx(eng, SEMS)
            for fn in pl.steps["sync"]:
                fn(ctx)

    return nc


def _host_prep(inputs):
    f = np.float32
    w = {}
    ip = np.asarray(inputs["L0_attn_in_proj_w"], f)
    ip_perm = np.concatenate([
        ip[0:4096][S_INV], ip[4096:5120][XV_PERM], ip[5120:6144][BM_PERM],
        ip[6144:10240][CM_PERM], ip[10240:10368]], axis=0)
    ipT = ip_perm.T
    w["inprojT"] = np.ascontiguousarray(
        ipT.reshape(16, 128, 81, 128).transpose(2, 1, 0, 3).reshape(81, 128, 2048))
    w["dtprojT"] = np.ascontiguousarray(np.asarray(inputs["L0_attn_dt_proj_w"], f)[S_INV].T)
    w["dtb"] = np.ascontiguousarray(np.asarray(inputs["L0_attn_dt_proj_b"], f)[S_INV].reshape(32, 128).T)
    w["Asb"] = np.ascontiguousarray(
        np.asarray(inputs["L0_attn_A_log"], f)[S_INV].reshape(32, 128, 16).transpose(1, 0, 2).reshape(128, 512))
    w["Dsk"] = np.ascontiguousarray(np.asarray(inputs["L0_attn_D_skip"], f)[S_INV].reshape(32, 128).T)
    w["rms1"] = np.ascontiguousarray(np.asarray(inputs["L0_attn_rms1_w"], f).reshape(16, 128).T)
    w["rms2"] = np.ascontiguousarray(np.asarray(inputs["L0_attn_rms2_w"], f).reshape(16, 128).T)
    opT = np.asarray(inputs["L0_attn_out_proj_w"], f)[:, S_INV].T
    w["outprojT"] = np.ascontiguousarray(
        opT.reshape(32, 128, 16, 128).transpose(2, 1, 0, 3).reshape(16, 128, 4096))
    gT = np.asarray(inputs["L0_attn_gate_w"], f).T
    uT = np.asarray(inputs["L0_attn_up_w"], f).T
    g4 = gT.reshape(16, 128, 64, 128).transpose(2, 1, 0, 3).reshape(64, 128, 2048)
    u4 = uT.reshape(16, 128, 64, 128).transpose(2, 1, 0, 3).reshape(64, 128, 2048)
    w["gupT"] = np.ascontiguousarray(np.concatenate([g4, u4], axis=2))
    dT = np.asarray(inputs["L0_attn_down_w"], f).T
    w["downT"] = np.ascontiguousarray(
        dT.reshape(4, 16, 128, 16, 128).transpose(0, 3, 2, 1, 4).reshape(4, 16, 128, 2048))
    w["WgT"] = np.ascontiguousarray(np.asarray(inputs["L0_ffn_Wg"], f).T)
    w1T = np.asarray(inputs["L0_ffn_W1"], f).transpose(0, 2, 1)
    w2T = np.asarray(inputs["L0_ffn_W2"], f).transpose(0, 2, 1)
    w1p = w1T.reshape(8, 16, 128, 11, 128).transpose(0, 3, 2, 1, 4).reshape(8, 11, 128, 2048)
    w2p = w2T.reshape(8, 16, 128, 11, 128).transpose(0, 3, 2, 1, 4).reshape(8, 11, 128, 2048)
    w["w12T"] = np.ascontiguousarray(np.concatenate([w1p, w2p], axis=3))
    w["w3T"] = np.ascontiguousarray(np.asarray(inputs["L0_ffn_W3"], f).transpose(0, 2, 1))
    w["ident"] = np.eye(128, dtype=f)
    oh = np.zeros((8, 1024), f)
    for e in range(8):
        oh[e, e * 128:(e + 1) * 128] = 1.0
    w["onehot"] = oh
    return w


def _umask_for(rk):
    pairs = [(0, 1), (0, 2), (0, 3), (1, 2), (1, 3), (2, 3)]
    v = np.zeros(15, np.float32)
    for j in range(3):
        v[j] = 1.0 if j < rk else 0.0
    for pi, (j, i) in enumerate(pairs):
        u = 1.0 if (j < i < rk) else 0.0
        v[3 + pi] = u
        v[9 + pi] = 1.0 - u
    return np.ascontiguousarray(np.broadcast_to(v[None, :], (128, 15)).copy())


def kernel(**inputs):
    x = np.asarray(inputs["L0_attn_hidden_states"], np.float32)
    w = _host_prep(inputs)
    nc = build_nc(debug=False)
    in_maps = []
    for i in range(8):
        b, q = i // 4, i % 4
        im = dict(w)
        im["xT"] = np.ascontiguousarray(x[b, q * TPC:(q + 1) * TPC].T)
        im["umask"] = _umask_for(q)
        in_maps.append(im)
    res = run_bass_kernel_spmd(nc, in_maps, core_ids=list(range(8)))
    out = np.empty((2, 8192, 2048), np.float32)
    for i in range(8):
        b, q = i // 4, i % 4
        out[b, q * TPC:(q + 1) * TPC] = res.results[i]["outT"].T
    return out


